# version 07
# baseline (speedup 1.0000x reference)
import os
import numpy as np
import ml_dtypes
import concourse.bass as bass
import concourse.mybir as mybir
from concourse.bass_utils import run_bass_kernel_spmd

F32 = mybir.dt.float32
BF16 = mybir.dt.bfloat16
AX = mybir.AxisListType.X
OP = mybir.AluOpType
ACT = mybir.ActivationFunctionType

MAGIC = 12582912.0
EPS = 1e-5
D = 2048
SEQ = 8192
NCORES = 8
CH = 512
TPC = 1024
NT = 8
KD = 16
SCALE = float(np.float32(1.0) / np.sqrt(np.float32(2048.0)))
NEG = -1e30

ENGS = ("T", "V", "S", "Y", "G")
PHASE_CAP = int(os.environ.get("MK_PHASE_CAP", "9"))


def chunk_loc(c):
    return (c, 0) if c < 8 else (15 - c, CH)


def own4(c):
    return [c, 15 - c, 16 + c, 31 - c]


def c256_loc(j):
    if j < 8:
        return j, 0
    if j < 16:
        return 15 - j, 256
    if j < 24:
        return j - 16, 512
    return 31 - j, 768


def build(phase_cap=PHASE_CAP):
    nc = bass.Bass("TRN2", target_bir_lowering=False)

    x_h = nc.dram_tensor("x_own", [TPC, D], F32, kind="ExternalInput")
    qkv_h = nc.dram_tensor("qkv", [D, 3 * D], F32, kind="ExternalInput")
    wo_h = nc.dram_tensor("o_proj", [D, D], F32, kind="ExternalInput")
    gamma_h = nc.dram_tensor("gamma", [D], F32, kind="ExternalInput")
    beta_h = nc.dram_tensor("beta", [D], F32, kind="ExternalInput")
    w_h = nc.dram_tensor("ffn_w", [D, D], F32, kind="ExternalInput")
    mask_h = nc.dram_tensor("mask_in", [NT, 128, SEQ], BF16, kind="ExternalInput")
    y_h = nc.dram_tensor("y_own", [TPC, D], F32, kind="ExternalOutput")

    kT_shard = nc.dram_tensor("kT_shard", [D, TPC], BF16)
    v_shard = nc.dram_tensor("v_shard", [TPC, D], BF16)
    kT_all = nc.dram_tensor("kT_all", [NCORES, D, TPC], BF16, addr_space="Shared")
    v_all = nc.dram_tensor("v_all", [NCORES, TPC, D], BF16, addr_space="Shared")
    q_dram = nc.dram_tensor("q_dram", [D, TPC], BF16)
    o_dram = nc.dram_tensor("o_dram", [TPC, D], BF16)
    wqT_dram = nc.dram_tensor("wqT_dram", [D, D], BF16)
    x_dram = nc.dram_tensor("x_dram", [TPC, D], F32)
    qkvb = nc.dram_tensor("qkvb", [D, 3 * D], BF16)
    wob = nc.dram_tensor("wob", [D, D], BF16)

    identb_h = nc.inline_tensor(np.eye(128, dtype=ml_dtypes.bfloat16), "identb")
    identf_h = nc.inline_tensor(np.eye(16, dtype=np.float32), "identf")
    ones_col_h = nc.inline_tensor(np.ones((128, 1), dtype=np.float32), "onec")
    ones_row_h = nc.inline_tensor(np.ones((1, 128), dtype=np.float32), "oner")
    scale_col_h = nc.inline_tensor(np.full((128, 1), SCALE, dtype=np.float32), "sccol")

    alloc = nc.alloc_sbuf_tensor
    identb = alloc("sb_identb", [128, 128], BF16)
    identf = alloc("sb_identf", [16, 16], F32)
    ones_col = alloc("sb_onec", [128, 1], F32)
    ones_row = alloc("sb_oner", [1, 128], F32)
    scale_col = alloc("sb_sccol", [128, 1], F32)
    gamma_fm = alloc("gamma_fm", [128, 16], F32)
    beta_fm = alloc("beta_fm", [128, 16], F32)
    g16 = alloc("g16", [16, 128], F32)
    b16 = alloc("b16", [16, 128], F32)
    wred = alloc("wred", [128, 16], F32)
    asum = alloc("asum", [128, 1], F32)
    wtot = alloc("wtot", [1, 1], F32)
    wsc1 = alloc("wsc1", [1, 1], F32)
    rws1 = alloc("rws1", [1, 1], F32)
    ws128 = alloc("ws128", [128, 1], F32)
    rws128 = alloc("rws128", [128, 1], F32)
    st_su = alloc("st_su", [128, NT], F32)
    st_mu = alloc("st_mu", [128, NT], F32)
    st_sq = alloc("st_sq", [128, NT], F32)
    st_ve = alloc("st_ve", [128, NT], F32)
    st_rs = alloc("st_rs", [128, NT], F32)
    st_am = alloc("st_am", [128, NT], F32)
    st_as = alloc("st_as", [128, NT], F32)
    st_ct = alloc("st_ct", [128, NT], F32)
    st_s2 = alloc("st_s2", [128, NT], F32)
    st_t1 = alloc("st_t1", [128, NT], F32)
    st_t2 = alloc("st_t2", [128, NT], F32)
    p3_m = alloc("p3_m", [128, 2], F32)
    p3_mx = alloc("p3_mx", [128, 2], F32)
    p3_nm = alloc("p3_nm", [128, 2], F32)
    p3_se = alloc("p3_se", [128, 2], F32)
    p3_rs = alloc("p3_rs", [128, 2], F32)

    x_sb = alloc("x_sb", [128, NT * D], F32)
    x_base = nc.lookup_mloc(x_sb).addr

    ARENA = 141 * 1024
    arena = alloc("arena", [128, ARENA], mybir.dt.uint8)
    abase = nc.lookup_mloc(arena).addr

    def at(name, shape, dtype, off, base=None, limit=None):
        b = abase if base is None else base
        lim = ARENA if base is None else limit
        per = int(np.prod(shape[1:])) * mybir.dt.size(dtype)
        assert lim is None or off + per <= lim, (name, off, per, lim)
        return nc.alloc_sbuf_tensor_at(name, shape, dtype, offset=b + off)

    wqT = at("wqT", [128, KD * D], BF16, 0)
    wstr = at("wstr", [128, 2 * D], F32, 65536)
    xm_t = at("xm_t", [128, D], F32, 81920)
    scr2 = at("scr2", [128, D], F32, 90112)
    xqb = at("xqb", [128, D], BF16, 98304)
    xqT_all = at("xqT_all", [128, NT * D], BF16, 102400)
    qtmp = at("qtmp", [128, D], F32, 135168)
    xnT = at("xnT", [128, KD * TPC], BF16, 0)
    wblk = at("wblk", [128, 4 * KD * 512], BF16, 32768)
    qkst = at("qkst", [128, 2 * 1024], BF16, 98304)
    p2xm = at("p2xm", [128, D], F32, 102400)
    p2scr = at("p2scr", [128, D], F32, 110592)
    p2tmpb = at("p2tmpb", [128, D], BF16, 118784)
    vstage = at("vstage", [128, NT * D], BF16, 0, base=x_base, limit=65536)
    qstage = at("qstage", [128, KD * TPC], BF16, 32768, base=x_base, limit=65536)
    S_a = at("S_a", [128, SEQ], F32, 0)
    S_b = at("S_b", [128, SEQ], F32, 32768)
    PT_a = at("PT_a", [128, SEQ], BF16, 0)
    PT_b = at("PT_b", [128, SEQ], BF16, 32768)
    P_a = at("P_a", [128, SEQ], BF16, 65536)
    P_b = at("P_b", [128, SEQ], BF16, 81920)
    kbuf = at("kbuf", [128, 2 * KD * 512], BF16, 98304)
    qTt = at("qTt", [128, KD * 256], BF16, 131072)
    ost = at("ost", [128, D], BF16, 139264)
    vbuf = at("vbuf", [128, 2 * 4 * D], BF16, 0, base=x_base, limit=65536)
    mb_a = at("mb_a", [128, SEQ], BF16, 32768, base=x_base, limit=65536)
    mb_b = at("mb_b", [128, SEQ], BF16, 49152, base=x_base, limit=65536)
    oT_all = at("oT_all", [128, NT * D], BF16, 0)
    ob = at("ob", [128, 2 * D], BF16, 32768)
    woblk = at("woblk", [128, 2 * KD * 512], BF16, 40960)
    p5qT = at("p5qT", [128, NT * D], BF16, 0)
    p5xm = at("p5xm", [128, D], F32, 32768)
    p5scr = at("p5scr", [128, D], F32, 40960)
    p5qb = at("p5qb", [128, D], BF16, 49152)
    p5wblk = at("p5wblk", [128, 2 * KD * 512], BF16, 53248)
    p5qtmp = at("p5qtmp", [128, D], F32, 86016)

    ps = nc.alloc_psum_tensor("ps", [128, 4096], F32)

    def bank(i, w=512):
        return ps[:, i * 512: i * 512 + w]

    sem_names = ["dY", "dG", "cc", "tT", "vV", "sS", "bar", "dOut"]
    sems = {}
    C = {n: 0 for n in sem_names}
    sched = {e: [] for e in ENGS}

    def E(eng, fn):
        sched[eng].append(fn)

    hwm = {}

    def W(eng, sem, val):
        if val and val > 0 and hwm.get((eng, sem), 0) < val:
            hwm[(eng, sem)] = val
            E(eng, lambda e, s=sem, v=val: e.wait_ge(sems[s], v))

    def DMA(eng, sem, out, in_):
        C[sem] += 16
        E(eng, lambda e, o=out, i=in_, s=sem: e.dma_start(out=o, in_=i).then_inc(sems[s], 16))
        return C[sem]

    def INC(eng, sem, fn):
        C[sem] += 1
        E(eng, lambda e, f=fn, s=sem: f(e).then_inc(sems[s], 1))
        return C[sem]

    def RAW(eng, fn):
        E(eng, fn)

    def DR(eng):
        E(eng, lambda e: e.drain())

    def BARRIER():
        tgt = C["bar"] + 5
        for eng in ENGS:
            if eng == "Y":
                W(eng, "dY", C["dY"])
                W(eng, "dOut", C["dOut"])
            if eng == "G":
                W(eng, "dG", C["dG"])
                W(eng, "cc", C["cc"])
            E(eng, lambda e: e.sem_inc(sems["bar"], 1))
            W(eng, "bar", tgt)
        C["bar"] = tgt

    bank_guard = {b: (None, 0) for b in range(8)}

    def use_banks(banks):
        for b in banks:
            sem, val = bank_guard[b]
            if sem is not None:
                W("T", sem, val)

    def release_banks(banks, sem, val):
        for b in banks:
            bank_guard[b] = (sem, val)

    jv = wred
    js = asum

    def SET_V(n=2):
        for _ in range(n):
            RAW("V", lambda e: e.tensor_copy(jv[:, 0:16], jv[:, 0:16]))

    def SET_S():
        RAW("S", lambda e: e.copy(js[:, :], js[:, :]))

    DMA("G", "dG", qkvb[:, :], qkv_h[:, :])
    v_conv = DMA("G", "dG", wob[:, :], wo_h[:, :])
    DMA("Y", "dY", identb[:, :], identb_h[:, :])
    DMA("Y", "dY", identf[:, :], identf_h[:, :])
    DMA("Y", "dY", ones_col[:, :], ones_col_h[:, :])
    DMA("Y", "dY", ones_row[:, :], ones_row_h[:, :])
    v_const = DMA("Y", "dY", scale_col[:, :], scale_col_h[:, :])
    DMA("Y", "dY", g16[:, :], gamma_h[:].rearrange("(a p) -> a p", p=128))
    v_g16 = DMA("Y", "dY", b16[:, :], beta_h[:].rearrange("(a p) -> a p", p=128))
    v_x = DMA("Y", "dY", x_sb[:, :].rearrange("p (a f) -> p a f", a=NT),
              x_h[:, :].rearrange("(a p) f -> p a f", p=128))

    W("T", "dY", v_g16)
    use_banks([6])
    t1 = INC("T", "tT", lambda e: e.matmul(bank(6, 16), g16[:, :], identf[:, :],
                                           start=True, stop=True))
    W("V", "tT", t1)
    vg = INC("V", "vV", lambda e: e.tensor_copy(gamma_fm[:, :], bank(6, 16)))
    release_banks([6], "vV", vg)
    use_banks([7])
    t2 = INC("T", "tT", lambda e: e.matmul(bank(7, 16), b16[:, :], identf[:, :],
                                           start=True, stop=True))
    W("V", "tT", t2)
    vb = INC("V", "vV", lambda e: e.tensor_copy(beta_fm[:, :], bank(7, 16)))
    release_banks([7], "vV", vb)

    w1 = {}
    for o in range(16):
        par = o % 2
        if o >= 2:
            W("Y", "sS", w1[o - 2])
        vd = DMA("Y", "dY", wstr[:, par * D:(par + 1) * D],
                 w_h[o * 128:(o + 1) * 128, :])
        W("S", "dY", vd)
        w1[o] = INC("S", "sS", lambda e, p=par, o_=o: e.activation(
            scr2[:, :], wstr[:, p * D:(p + 1) * D], ACT.Abs,
            accum_out=wred[:, o_:o_ + 1]))
        DR("S")
    W("V", "sS", w1[15])
    v_as = INC("V", "vV", lambda e: e.tensor_reduce(asum[:, :], wred[:, :], AX, OP.add))
    DR("V")
    W("T", "vV", v_as)
    use_banks([0])
    tt = INC("T", "tT", lambda e: e.matmul(ps[0:1, 0:1], asum[:, :], ones_col[:, :],
                                           start=True, stop=True))
    W("V", "tT", tt)
    v0 = INC("V", "vV", lambda e: e.tensor_copy(wtot[:, :], ps[0:1, 0:1]))
    release_banks([0], "vV", v0)
    DR("V")
    v_ws = INC("V", "vV", lambda e: e.tensor_scalar(
        wsc1[:, :], wtot[:, :], 1.0 / float(D * D), 1e-6, OP.mult, OP.max))
    DR("V")
    v_rw = INC("V", "vV", lambda e: e.reciprocal(rws1[:, :], wsc1[:, :]))
    DR("V")
    W("T", "vV", v_rw)
    use_banks([1])
    tt = INC("T", "tT", lambda e: e.matmul(bank(1)[:, 0:1], ones_row[:, :], wsc1[:, :],
                                           start=True, stop=True))
    W("V", "tT", tt)
    v_b1 = INC("V", "vV", lambda e: e.tensor_copy(ws128[:, :], bank(1)[:, 0:1]))
    release_banks([1], "vV", v_b1)
    use_banks([2])
    tt = INC("T", "tT", lambda e: e.matmul(bank(2)[:, 0:1], ones_row[:, :], rws1[:, :],
                                           start=True, stop=True))
    W("V", "tT", tt)
    v_b2 = INC("V", "vV", lambda e: e.tensor_copy(rws128[:, :], bank(2)[:, 0:1]))
    release_banks([2], "vV", v_b2)
    DR("V")


    def TMM(sig, fn):
        if sig:
            return INC("T", "tT", fn)
        RAW("T", fn)
        return None

    tp_last_mm = [0]

    def transpose128(src_ap_fn, n, evac_fn, wait_v=0, banks=(6, 7), wait_s=0,
                     evac_eng="V"):
        last = 0
        esem = "vV" if evac_eng == "V" else "sS"
        for k in range(n):
            b = banks[k % 2]
            if k == 0:
                W("T", "vV", wait_v)
                W("T", "sS", wait_s)
            use_banks([b])
            tmm = INC("T", "tT", lambda e, k_=k, b_=b: e.matmul(
                ps[:, b_ * 512: b_ * 512 + 128], src_ap_fn(k_), identb[:, :],
                start=True, stop=True))
            W(evac_eng, "tT", tmm)
            if evac_eng == "V":
                SET_V(1)
            else:
                SET_S()
            last = INC(evac_eng, esem, lambda e, k_=k, b_=b: evac_fn(
                e, k_, ps[:, b_ * 512: b_ * 512 + 128]))
            release_banks([b], esem, last)
        tp_last_mm[0] = C["tT"]
        return last

    wq_done = {15: 0}
    gq_done = {}
    for o in range(16):
        par = o % 2
        if o >= 2:
            W("Y", "vV", gq_done[o - 2])
        else:
            W("Y", "sS", w1[15])
        vd = DMA("Y", "dY", wstr[:, par * D:(par + 1) * D],
                 w_h[o * 128:(o + 1) * 128, :])
        W("V", "dY", vd)
        W("V", "vV", v_b2)
        W("V", "tT", tp_last_mm[0])
        RAW("V", lambda e, p=par: e.tensor_scalar(
            scr2[:, :], wstr[:, p * D:(p + 1) * D], rws128[:, :], MAGIC,
            OP.mult, OP.add))
        RAW("V", lambda e: e.tensor_scalar(scr2[:, :], scr2[:, :], MAGIC, 1.0,
                                           OP.subtract, OP.min))
        v_q = INC("V", "vV", lambda e: e.tensor_scalar(
            xqb[:, :], scr2[:, :], -1.0, None, OP.max))
        gq_done[o] = v_q
        wq_done[o] = transpose128(
            lambda k: xqb[:, k * 128:(k + 1) * 128], 16,
            lambda e, k, psl, o_=o: e.copy(
                wqT[:, k * D + o_ * 128: k * D + o_ * 128 + 128], psl),
            wait_v=v_q, evac_eng="S")
    v_wqT = wq_done[15]
    W("Y", "sS", v_wqT)
    for k in range(16):
        DMA("Y", "dY", wqT_dram[k * 128:(k + 1) * 128, :], wqT[:, k * D:(k + 1) * D])

    def ln_stats(i, xmbuf, scrbuf, extra_inner=False):
        RAW("V", lambda e, i_=i: e.tensor_reduce(
            st_su[:, i_:i_ + 1], x_sb[:, i_ * D:(i_ + 1) * D], AX, OP.add))
        DR("V")
        RAW("V", lambda e, i_=i: e.tensor_scalar(
            st_mu[:, i_:i_ + 1], st_su[:, i_:i_ + 1], 1.0 / D, None, OP.mult))
        DR("V")
        v_xm = INC("V", "vV", lambda e, i_=i, xb=xmbuf: e.tensor_scalar(
            xb[:, :], x_sb[:, i_ * D:(i_ + 1) * D], st_mu[:, i_:i_ + 1], None,
            OP.subtract))
        W("S", "vV", v_xm)
        s_sq = INC("S", "sS", lambda e, i_=i, xb=xmbuf, sc=scrbuf: e.activation(
            sc[:, :], xb[:, :], ACT.Square, accum_out=st_sq[:, i_:i_ + 1]))
        DR("S")
        W("V", "sS", s_sq)
        v_ve = INC("V", "vV", lambda e, i_=i: e.tensor_scalar(
            st_ve[:, i_:i_ + 1], st_sq[:, i_:i_ + 1], 1.0 / D, EPS, OP.mult, OP.add))
        DR("V")
        W("S", "vV", v_ve)
        RAW("S", lambda e, i_=i: e.activation(st_t1[:, i_:i_ + 1], st_ve[:, i_:i_ + 1],
                                              ACT.Ln))
        DR("S")
        s_rs = INC("S", "sS", lambda e, i_=i: e.activation(
            st_rs[:, i_:i_ + 1], st_t1[:, i_:i_ + 1], ACT.Exp, scale=-0.5))
        DR("S")
        if not extra_inner:
            return s_rs
        W("V", "sS", s_rs)
        RAW("V", lambda e, i_=i: e.tensor_scalar(
            st_t2[:, i_:i_ + 1], st_ve[:, i_:i_ + 1], EPS, st_rs[:, i_:i_ + 1],
            OP.subtract, OP.mult))
        DR("V")
        v_t2 = INC("V", "vV", lambda e, i_=i: e.tensor_scalar(
            st_t2[:, i_:i_ + 1], st_t2[:, i_:i_ + 1], st_rs[:, i_:i_ + 1], EPS,
            OP.mult, OP.add))
        DR("V")
        W("S", "vV", v_t2)
        RAW("S", lambda e, i_=i: e.activation(st_t1[:, i_:i_ + 1],
                                              st_t2[:, i_:i_ + 1], ACT.Ln))
        DR("S")
        s_c2 = INC("S", "sS", lambda e, i_=i: e.activation(
            st_s2[:, i_:i_ + 1], st_t1[:, i_:i_ + 1], ACT.Exp, scale=-0.5))
        DR("S")
        W("V", "sS", s_c2)
        v_s2 = INC("V", "vV", lambda e, i_=i: e.tensor_scalar(
            st_s2[:, i_:i_ + 1], st_s2[:, i_:i_ + 1], st_rs[:, i_:i_ + 1], None,
            OP.mult))
        DR("V")
        return v_s2

    def quantize_tile(i, xmbuf, scrbuf, qbuf, qT_all_buf, s_rs):
        INC("V", "vV", lambda e, i_=i, xb=xmbuf: e.tensor_reduce(
            st_am[:, i_:i_ + 1], xb[:, :], AX, OP.max, apply_absolute_value=True))
        DR("V")
        W("V", "sS", s_rs)
        RAW("V", lambda e, i_=i: e.tensor_scalar(
            st_as[:, i_:i_ + 1], st_am[:, i_:i_ + 1], st_rs[:, i_:i_ + 1], 1e-8,
            OP.mult, OP.max))
        DR("V")
        RAW("V", lambda e, i_=i: e.reciprocal(st_t2[:, i_:i_ + 1], st_as[:, i_:i_ + 1]))
        DR("V")
        RAW("V", lambda e, i_=i: e.tensor_scalar(
            st_t1[:, i_:i_ + 1], st_rs[:, i_:i_ + 1], 127.0, st_t2[:, i_:i_ + 1],
            OP.mult, OP.mult))
        DR("V")
        RAW("V", lambda e, i_=i: e.scalar_tensor_tensor(
            st_ct[:, i_:i_ + 1], st_as[:, i_:i_ + 1], 0.5 / 127.0, ws128[:, :],
            OP.mult, OP.mult))
        DR("V")
        RAW("V", lambda e, i_=i, xb=xmbuf, sc=scrbuf: e.tensor_scalar(
            sc[:, :], xb[:, :], st_t1[:, i_:i_ + 1], MAGIC, OP.mult, OP.add))
        RAW("V", lambda e, sc=scrbuf: e.tensor_scalar(
            sc[:, :], sc[:, :], MAGIC, 127.0, OP.subtract, OP.min))
        W("V", "tT", tp_last_mm[0])
        v_q = INC("V", "vV", lambda e, sc=scrbuf, qb=qbuf: e.tensor_scalar(
            qb[:, :], sc[:, :], -127.0, None, OP.max))
        return transpose128(
            lambda k, qb=qbuf: qb[:, k * 128:(k + 1) * 128], 16,
            lambda e, k, psl, i_=i, buf=qT_all_buf: e.tensor_copy(
                buf[:, i_ * D + k * 128: i_ * D + k * 128 + 128], psl),
            wait_v=v_q)

    def ln_quant_fast(i, scrbuf, qbuf, qT_all_buf):
        RAW("V", lambda e, i_=i: e.tensor_reduce(
            st_su[:, i_:i_ + 1], x_sb[:, i_ * D:(i_ + 1) * D], AX, OP.add,
            negate=True))
        DR("V")
        v_nmu = INC("V", "vV", lambda e, i_=i: e.tensor_scalar(
            st_mu[:, i_:i_ + 1], st_su[:, i_:i_ + 1], 1.0 / D, None, OP.mult))
        DR("V")
        W("S", "vV", v_nmu)
        s_sq = INC("S", "sS", lambda e, i_=i, sc=scrbuf: e.activation(
            sc[:, :], x_sb[:, i_ * D:(i_ + 1) * D], ACT.Square,
            bias=st_mu[:, i_:i_ + 1], accum_out=st_sq[:, i_:i_ + 1]))
        DR("S")
        W("V", "sS", s_sq)
        v_ve = INC("V", "vV", lambda e, i_=i: e.tensor_scalar(
            st_ve[:, i_:i_ + 1], st_sq[:, i_:i_ + 1], 1.0 / D, EPS, OP.mult, OP.add))
        DR("V")
        v_mx = INC("V", "vV", lambda e, i_=i, sc=scrbuf: e.tensor_reduce(
            st_am[:, i_:i_ + 1], sc[:, :], AX, OP.max))
        DR("V")
        W("S", "vV", v_ve)
        RAW("S", lambda e, i_=i: e.activation(st_t1[:, i_:i_ + 1], st_ve[:, i_:i_ + 1],
                                              ACT.Ln))
        DR("S")
        RAW("S", lambda e, i_=i: e.activation(
            st_rs[:, i_:i_ + 1], st_t1[:, i_:i_ + 1], ACT.Exp, scale=-0.5))
        DR("S")
        W("S", "vV", v_mx)
        RAW("S", lambda e, i_=i: e.activation(st_t2[:, i_:i_ + 1], st_am[:, i_:i_ + 1],
                                              ACT.Ln))
        DR("S")
        s_am = INC("S", "sS", lambda e, i_=i: e.activation(
            st_am[:, i_:i_ + 1], st_t2[:, i_:i_ + 1], ACT.Exp, scale=0.5))
        DR("S")
        W("V", "sS", s_am)
        RAW("V", lambda e, i_=i: e.tensor_scalar(
            st_as[:, i_:i_ + 1], st_am[:, i_:i_ + 1], st_rs[:, i_:i_ + 1], 1e-8,
            OP.mult, OP.max))
        DR("V")
        RAW("V", lambda e, i_=i: e.reciprocal(st_t2[:, i_:i_ + 1], st_as[:, i_:i_ + 1]))
        DR("V")
        RAW("V", lambda e, i_=i: e.tensor_scalar(
            st_t1[:, i_:i_ + 1], st_rs[:, i_:i_ + 1], 127.0, st_t2[:, i_:i_ + 1],
            OP.mult, OP.mult))
        DR("V")
        RAW("V", lambda e, i_=i: e.tensor_scalar(
            st_t2[:, i_:i_ + 1], st_mu[:, i_:i_ + 1], st_t1[:, i_:i_ + 1], None,
            OP.mult))
        DR("V")
        RAW("V", lambda e, i_=i: e.scalar_tensor_tensor(
            st_ct[:, i_:i_ + 1], st_as[:, i_:i_ + 1], 0.5 / 127.0, ws128[:, :],
            OP.mult, OP.mult))
        DR("V")
        RAW("V", lambda e, i_=i, sc=scrbuf: e.tensor_scalar(
            sc[:, :], x_sb[:, i_ * D:(i_ + 1) * D], st_t1[:, i_:i_ + 1],
            st_t2[:, i_:i_ + 1], OP.mult, OP.add))
        RAW("V", lambda e, sc=scrbuf: e.tensor_scalar(
            sc[:, :], sc[:, :], MAGIC, MAGIC, OP.add, OP.subtract))
        W("V", "tT", tp_last_mm[0])
        v_q = INC("V", "vV", lambda e, sc=scrbuf, qb=qbuf: e.tensor_scalar(
            qb[:, :], sc[:, :], 127.0, -127.0, OP.min, OP.max))
        return transpose128(
            lambda k, qb=qbuf: qb[:, k * 128:(k + 1) * 128], 16,
            lambda e, k, psl, i_=i, buf=qT_all_buf: e.copy(
                buf[:, i_ * D + k * 128: i_ * D + k * 128 + 128], psl),
            wait_v=v_q, evac_eng="S")

    def ln_stats_group(tiles, bufs, v_mx, s_sq):
        for i in tiles:
            RAW("V", lambda e, i_=i: e.tensor_reduce(
                st_su[:, i_:i_ + 1], x_sb[:, i_ * D:(i_ + 1) * D], AX, OP.add,
                negate=True))
        DR("V")
        v_nmu = 0
        for i in tiles:
            v_nmu = INC("V", "vV", lambda e, i_=i: e.tensor_scalar(
                st_mu[:, i_:i_ + 1], st_su[:, i_:i_ + 1], 1.0 / D, None, OP.mult))
        DR("V")
        for i in tiles:
            W("S", "vV", v_nmu)
            if i - 2 in v_mx:
                W("S", "vV", v_mx[i - 2])
            s_sq[i] = INC("S", "sS", lambda e, i_=i, sc=bufs[i % 2]: e.activation(
                sc[:, :], x_sb[:, i_ * D:(i_ + 1) * D], ACT.Square,
                bias=st_mu[:, i_:i_ + 1], accum_out=st_sq[:, i_:i_ + 1]))
            DR("S")
            W("V", "sS", s_sq[i])
            RAW("V", lambda e, i_=i: e.tensor_scalar(
                st_ve[:, i_:i_ + 1], st_sq[:, i_:i_ + 1], 1.0 / D, EPS,
                OP.mult, OP.add))
            v_mx[i] = INC("V", "vV", lambda e, i_=i, sc=bufs[i % 2]: e.tensor_reduce(
                st_am[:, i_:i_ + 1], sc[:, :], AX, OP.max))
        DR("V")
        v_ve = C["vV"]
        for i in tiles:
            W("S", "vV", v_ve)
            RAW("S", lambda e, i_=i: e.activation(st_t1[:, i_:i_ + 1],
                                                  st_ve[:, i_:i_ + 1], ACT.Ln))
        DR("S")
        for i in tiles:
            RAW("S", lambda e, i_=i: e.activation(
                st_rs[:, i_:i_ + 1], st_t1[:, i_:i_ + 1], ACT.Exp, scale=-0.5))
        DR("S")
        for i in tiles:
            RAW("S", lambda e, i_=i: e.activation(st_t2[:, i_:i_ + 1],
                                                  st_am[:, i_:i_ + 1], ACT.Ln))
        DR("S")
        s_fin = 0
        for i in tiles:
            s_fin = INC("S", "sS", lambda e, i_=i: e.activation(
                st_am[:, i_:i_ + 1], st_t2[:, i_:i_ + 1], ACT.Exp, scale=0.5))
        DR("S")
        W("V", "sS", s_fin)
        for i in tiles:
            RAW("V", lambda e, i_=i: e.tensor_scalar(
                st_as[:, i_:i_ + 1], st_am[:, i_:i_ + 1], st_rs[:, i_:i_ + 1], 1e-8,
                OP.mult, OP.max))
        DR("V")
        for i in tiles:
            RAW("V", lambda e, i_=i: e.reciprocal(st_t2[:, i_:i_ + 1],
                                                  st_as[:, i_:i_ + 1]))
        DR("V")
        for i in tiles:
            RAW("V", lambda e, i_=i: e.tensor_scalar(
                st_t1[:, i_:i_ + 1], st_rs[:, i_:i_ + 1], 127.0, st_t2[:, i_:i_ + 1],
                OP.mult, OP.mult))
        DR("V")
        for i in tiles:
            RAW("V", lambda e, i_=i: e.tensor_scalar(
                st_su[:, i_:i_ + 1], st_mu[:, i_:i_ + 1], st_t1[:, i_:i_ + 1], None,
                OP.mult))
        for i in tiles:
            RAW("V", lambda e, i_=i: e.scalar_tensor_tensor(
                st_ct[:, i_:i_ + 1], st_as[:, i_:i_ + 1], 0.5 / 127.0, ws128[:, :],
                OP.mult, OP.mult))
        DR("V")
        return INC("V", "vV", lambda e: e.engine_nop())

    def quant_tile_fast(i, qt, qbuf, qT_all_buf, v_stats):
        W("V", "vV", v_stats)
        RAW("V", lambda e, i_=i, sc=qt: e.tensor_scalar(
            sc[:, :], x_sb[:, i_ * D:(i_ + 1) * D], st_t1[:, i_:i_ + 1],
            st_su[:, i_:i_ + 1], OP.mult, OP.add))
        RAW("V", lambda e, sc=qt: e.tensor_scalar(
            sc[:, :], sc[:, :], MAGIC, MAGIC, OP.add, OP.subtract))
        W("V", "tT", tp_last_mm[0])
        v_q = INC("V", "vV", lambda e, sc=qt, qb=qbuf: e.tensor_scalar(
            qb[:, :], sc[:, :], 127.0, -127.0, OP.min, OP.max))
        return transpose128(
            lambda k, qb=qbuf: qb[:, k * 128:(k + 1) * 128], 16,
            lambda e, k, psl, i_=i, buf=qT_all_buf: e.copy(
                buf[:, i_ * D + k * 128: i_ * D + k * 128 + 128], psl),
            wait_v=v_q, evac_eng="S")

    if phase_cap >= 1:
        W("V", "dY", v_x)
        W("S", "dY", v_x)
        v_mx1, s_sq1 = {}, {}
        grp_stats = {}
        grp_stats[0] = ln_stats_group([0, 1], (xm_t, scr2), v_mx1, s_sq1)
        for i in range(NT):
            ready = quant_tile_fast(i, qtmp, xqb, xqT_all, grp_stats[i // 2])
            if i % 2 == 1 and i + 1 < NT:
                grp_stats[i // 2 + 1] = ln_stats_group(
                    [i + 1, i + 2], (xm_t, scr2), v_mx1, s_sq1)
            bk = i % 2
            W("T", "sS", ready)
            for nb in range(4):
                use_banks([bk])
                for k in range(16):
                    tmm = TMM(k == 15, lambda e, b_=bk, i_=i, k_=k, nb_=nb,
                              st=(k == 0), sp=(k == 15): e.matmul(
                        bank(b_),
                        xqT_all[:, i_ * D + k_ * 128: i_ * D + k_ * 128 + 128],
                        wqT[:, k_ * D + nb_ * 512: k_ * D + nb_ * 512 + 512],
                        start=st, stop=sp))
                W("V", "tT", tmm)
                SET_V(1)
                ve = INC("V", "vV", lambda e, b_=bk, i_=i, nb_=nb:
                         e.scalar_tensor_tensor(
                    x_sb[:, i_ * D + nb_ * 512: i_ * D + nb_ * 512 + 512], bank(b_),
                    st_ct[:, i_:i_ + 1],
                    x_sb[:, i_ * D + nb_ * 512: i_ * D + nb_ * 512 + 512],
                    OP.mult, OP.add))
                release_banks([bk], "vV", ve)

    BARRIER()

    cc_k = cc_v = 0
    if phase_cap >= 2:
        v_xnT = 0
        for i in range(NT):
            v_s2 = ln_stats(i, p2xm, p2scr, extra_inner=True)
            W("V", "tT", tp_last_mm[0])
            v_tmpb = INC("V", "vV", lambda e, i_=i: e.tensor_scalar(
                p2tmpb[:, :], p2xm[:, :], st_s2[:, i_:i_ + 1], None, OP.mult))
            INC("V", "vV", lambda e, i_=i: e.scalar_tensor_tensor(
                x_sb[:, i_ * D:(i_ + 1) * D], p2xm[:, :], st_rs[:, i_:i_ + 1],
                x_sb[:, i_ * D:(i_ + 1) * D], OP.mult, OP.add))
            v_xnT = transpose128(
                lambda k: p2tmpb[:, k * 128:(k + 1) * 128], 16,
                lambda e, k, psl, i_=i: e.tensor_scalar(
                    xnT[:, k * TPC + i_ * 128: k * TPC + i_ * 128 + 128], psl,
                    gamma_fm[:, k:k + 1], beta_fm[:, k:k + 1], OP.mult, OP.add),
                wait_v=v_tmpb)

        W("Y", "vV", C["vV"])
        v_xpage = DMA("Y", "dY", x_dram[:, :].rearrange("(a p) f -> p a f", p=128),
                      x_sb[:, :].rearrange("p (a f) -> p a f", a=NT))

        stage_free = [0, 0]
        wblk_free = [0, 0]

        def qk_block(m, row, out_dram):
            par = m % 2
            W("Y", "tT", wblk_free[par])
            W("Y", "dG", v_conv)
            vd = DMA("Y", "dY",
                     wblk[:, par * KD * 512: par * KD * 512 + KD * 128]
                     .rearrange("p (a f) -> p a f", a=KD),
                     qkvb[:, m * 128:(m + 1) * 128]
                     .rearrange("(a p) f -> p a f", p=128))
            W("T", "dY", vd)
            W("T", "vV", v_xnT)
            banks2 = (0, 1) if par == 0 else (2, 3)
            use_banks(banks2)
            tmm = 0
            for h in range(2):
                for k in range(16):
                    tmm = TMM(h == 1 and k == 15, lambda e, k_=k, h_=h, p_=par,
                              st=(k == 0), sp=(k == 15): e.matmul(
                        ps[:, p_ * 1024 + h_ * 512: p_ * 1024 + h_ * 512 + 512],
                        wblk[:, p_ * KD * 512 + k_ * 128: p_ * KD * 512 + k_ * 128 + 128],
                        xnT[:, k_ * TPC + h_ * 512: k_ * TPC + h_ * 512 + 512],
                        start=st, stop=sp))
            wblk_free[par] = tmm
            W("S", "tT", tmm)
            W("S", "dY", stage_free[par])
            SET_S()
            se = INC("S", "sS", lambda e, p_=par: e.copy(
                qkst[:, p_ * 1024:(p_ + 1) * 1024], ps[:, p_ * 1024:(p_ + 1) * 1024]))
            release_banks(banks2, "sS", se)
            W("Y", "sS", se)
            vy = DMA("Y", "dY", out_dram[row * 128:(row + 1) * 128, :],
                     qkst[:, par * 1024:(par + 1) * 1024])
            stage_free[par] = vy
            return vy

        for m in range(16):
            qk_block(16 + m, m, kT_shard)

        wv_loads = 0
        for nb in range(4):
            W("Y", "tT", wblk_free[nb % 2] if nb < 2 else 0)
            wv_loads = DMA("Y", "dY",
                           wblk[:, nb * KD * 512:(nb + 1) * KD * 512]
                           .rearrange("p (a f) -> p a f", a=KD),
                           qkvb[:, 2 * D + nb * 512: 2 * D + nb * 512 + 512]
                           .rearrange("(a p) f -> p a f", p=128))

        if os.environ.get("MK_SKIP_AG") != "1":
            W("G", "dY", C["dY"])
            W("G", "dG", C["dG"])
            C["cc"] += 1
            cc_k = C["cc"]
            E("G", lambda e: e.collective_compute(
                "AllGather", OP.bypass, replica_groups=[list(range(NCORES))],
                ins=[kT_shard[:, :]], outs=[kT_all[:, :, :]]).then_inc(sems["cc"], 1))

        for nb in range(4):
            for i in range(NT):
                bk = i % 2
                W("T", "dY", wv_loads)
                W("T", "vV", v_xnT)
                use_banks([bk])
                for k in range(16):
                    tmm = TMM(k == 15, lambda e, k_=k, i_=i, nb_=nb, b_=bk,
                              st=(k == 0), sp=(k == 15): e.matmul(
                        bank(b_),
                        xnT[:, k_ * TPC + i_ * 128: k_ * TPC + i_ * 128 + 128],
                        wblk[:, nb_ * KD * 512 + k_ * 512: nb_ * KD * 512 + k_ * 512 + 512],
                        start=st, stop=sp))
                W("S", "tT", tmm)
                W("S", "dY", v_xpage)
                SET_S()
                se = INC("S", "sS", lambda e, b_=bk, i_=i, nb_=nb: e.copy(
                    vstage[:, i_ * D + nb_ * 512: i_ * D + nb_ * 512 + 512], bank(b_)))
                release_banks([bk], "sS", se)
        v_vdone = C["sS"]

        W("Y", "sS", v_vdone)
        W("Y", "cc", cc_k)
        v_vst = DMA("Y", "dY", v_shard[:, :].rearrange("(a p) f -> p a f", p=128),
                    vstage[:, :].rearrange("p (a f) -> p a f", a=NT))
        wq_loads = 0
        for nb in range(4):
            W("Y", "cc", cc_k)
            W("Y", "tT", C["tT"])
            wq_loads = DMA("Y", "dY",
                           wblk[:, nb * KD * 512:(nb + 1) * KD * 512]
                           .rearrange("p (a f) -> p a f", a=KD),
                           qkvb[:, nb * 512: nb * 512 + 512]
                           .rearrange("(a p) f -> p a f", p=128))

        if os.environ.get("MK_SKIP_AG") != "1":
            W("G", "dY", C["dY"])
            W("G", "dG", C["dG"])
            C["cc"] += 1
            cc_v = C["cc"]
            E("G", lambda e: e.collective_compute(
                "AllGather", OP.bypass, replica_groups=[list(range(NCORES))],
                ins=[v_shard[:, :]], outs=[v_all[:, :, :]]).then_inc(sems["cc"], 1))

        for m in range(16):
            par = m % 2
            banks2 = (0, 1) if par == 0 else (2, 3)
            W("T", "dY", wq_loads)
            use_banks(banks2)
            for h in range(2):
                for k in range(16):
                    tmm = TMM(h == 1 and k == 15, lambda e, k_=k, h_=h, m_=m, p_=par,
                              st=(k == 0), sp=(k == 15): e.matmul(
                        ps[:, p_ * 1024 + h_ * 512: p_ * 1024 + h_ * 512 + 512],
                        wblk[:, (m_ // 4) * KD * 512 + k_ * 512 + (m_ % 4) * 128:
                             (m_ // 4) * KD * 512 + k_ * 512 + (m_ % 4) * 128 + 128],
                        xnT[:, k_ * TPC + h_ * 512: k_ * TPC + h_ * 512 + 512],
                        start=st, stop=sp))
            W("S", "tT", tmm)
            SET_S()
            se = INC("S", "sS", lambda e, m_=m, p_=par: e.copy(
                qstage[:, m_ * TPC:(m_ + 1) * TPC], ps[:, p_ * 1024:(p_ + 1) * 1024]))
            release_banks(banks2, "sS", se)

        W("Y", "sS", C["sS"])
        W("Y", "cc", cc_v)
        DMA("Y", "dY", q_dram[:, :].rearrange("(a p) f -> p a f", p=128),
            qstage[:, :].rearrange("p (a f) -> p a f", a=KD))

    BARRIER()

    if phase_cap >= 3:
        pairs = [(0, 1, 4), (2, 3, 8), (4, 5, 12), (6, 7, 16)]
        kbuf_free = [0, 0]
        vbuf_free = [0, 0]
        qTt_free = 0
        mb_free = 0
        ost_free = 0
        prev_pair_done = 0

        for (ja, jb, KF) in pairs:
            W("Y", "tT", qTt_free)
            vq = DMA("Y", "dY", qTt[:, :].rearrange("p (a f) -> p a f", a=KD),
                     q_dram[:, ja * 128: ja * 128 + 256]
                     .rearrange("(a p) f -> p a f", p=128))
            W("Y", "sS", mb_free)
            vma = DMA("Y", "dY", mb_a[:, 0:KF * 512], mask_h[ja, :, 0:KF * 512])
            vmb = DMA("Y", "dY", mb_b[:, 0:KF * 512], mask_h[jb, :, 0:KF * 512])
            kd_loaded = {}
            sevac_hist = {}
            for kc in range(KF):
                par = kc % 2
                for kk in (kc, kc + 1):
                    if kk < KF and kk not in kd_loaded:
                        W("Y", "tT", kbuf_free[kk % 2])
                        W("Y", "cc", cc_v)
                        for half in range(2):
                            r, off = c256_loc(2 * kk + half)
                            kd_loaded[kk] = DMA(
                                "Y", "dY",
                                kbuf[:, (kk % 2) * KD * 512:
                                     ((kk % 2) + 1) * KD * 512]
                                .rearrange("p (a f) -> p a f", f=512)
                                [:, :, half * 256:(half + 1) * 256],
                                kT_all[r, :, off:off + 256]
                                .rearrange("(a p) f -> p a f", p=128))
                W("T", "dY", kd_loaded[kc])
                W("T", "dY", max(vq, vma, vmb))
                for (tile, qoff, mb, psb) in ((0, 0, mb_a, 0), (1, 128, mb_b, 4)):
                    bk = psb + par
                    use_banks([bk])
                    for k in range(16):
                        RAW("T", lambda e, k_=k, b_=bk, q_=qoff, p_=par: e.matmul(
                            bank(b_),
                            qTt[:, k_ * 256 + q_: k_ * 256 + q_ + 128],
                            kbuf[:, p_ * KD * 512 + k_ * 512: p_ * KD * 512 + k_ * 512 + 512],
                            start=(k_ == 0), stop=False))
                    tmm = INC("T", "tT", lambda e, b_=bk, m_=mb, kc_=kc: e.matmul(
                        bank(b_), identb[:, :], m_[:, kc_ * 512:(kc_ + 1) * 512],
                        start=False, stop=True))
                    if tile == 1:
                        kbuf_free[par] = tmm
                        if kc == KF - 1:
                            qTt_free = tmm
                    Sbuf = S_a if tile == 0 else S_b
                    W("S", "tT", tmm)
                    SET_S()
                    se = INC("S", "sS", lambda e, b_=bk, kc_=kc, S_=Sbuf: e.copy(
                        S_[:, kc_ * 512:(kc_ + 1) * 512], bank(b_)))
                    release_banks([bk], "sS", se)
                    sevac_hist[(tile, kc)] = se
                    W("V", "sS", se)
                    if kc == 0:
                        INC("V", "vV", lambda e, S_=Sbuf, t_=tile: e.tensor_reduce(
                            p3_m[:, t_:t_ + 1], S_[:, 0:512], AX, OP.max))
                        DR("V")
                    else:
                        RAW("V", lambda e, kc_=kc, S_=Sbuf, t_=tile: e.tensor_reduce(
                            p3_mx[:, t_:t_ + 1], S_[:, kc_ * 512:(kc_ + 1) * 512],
                            AX, OP.max))
                        DR("V")
                        INC("V", "vV", lambda e, t_=tile: e.tensor_max(
                            p3_m[:, t_:t_ + 1], p3_m[:, t_:t_ + 1],
                            p3_mx[:, t_:t_ + 1]))
                        DR("V")
            mb_free = sevac_hist[(1, KF - 1)]
            v_nm = INC("V", "vV", lambda e: e.tensor_scalar(
                p3_nm[:, :], p3_m[:, :], -SCALE, None, OP.mult))
            DR("V")
            W("S", "vV", v_nm)
            RAW("S", lambda e, KF_=KF: e.activation(
                P_a[:, 0:KF_ * 512], S_a[:, 0:KF_ * 512], ACT.Exp,
                bias=p3_nm[:, 0:1], scale=scale_col[:, :], accum_out=p3_se[:, 0:1]))
            DR("S")
            s_expb = INC("S", "sS", lambda e, KF_=KF: e.activation(
                P_b[:, 0:KF_ * 512], S_b[:, 0:KF_ * 512], ACT.Exp,
                bias=p3_nm[:, 1:2], scale=scale_col[:, :], accum_out=p3_se[:, 1:2]))
            DR("S")
            W("V", "sS", s_expb)
            v_rs = INC("V", "vV", lambda e: e.reciprocal(p3_rs[:, :], p3_se[:, :]))
            DR("V")
            nblk = KF * 4
            v_pt = 0
            v_pt_kc = {}
            for m in range(2 * nblk):
                tile = m % 2
                blk = m // 2
                Pb = P_a if tile == 0 else P_b
                PTb = PT_a if tile == 0 else PT_b
                b8 = m % 8
                if m == 0:
                    W("T", "sS", s_expb)
                    W("T", "vV", prev_pair_done)
                use_banks([b8])
                tmm = INC("T", "tT", lambda e, b_=b8, P_=Pb, blk_=blk: e.matmul(
                    ps[:, b_ * 512: b_ * 512 + 128],
                    P_[:, blk_ * 128:(blk_ + 1) * 128], identb[:, :],
                    start=True, stop=True))
                W("V", "tT", tmm)
                SET_V(1)
                v_pt = INC("V", "vV", lambda e, b_=b8, PT_=PTb, blk_=blk:
                           e.tensor_copy(
                    PT_[:, blk_ * 128:(blk_ + 1) * 128],
                    ps[:, b_ * 512: b_ * 512 + 128]))
                release_banks([b8], "vV", v_pt)
                v_pt_kc[blk // 4] = v_pt
            vd_loaded = {}
            tmm_last = {0: 0, 1: 0}
            for kc in range(KF):
                par = kc % 2
                for kk in (kc, kc + 1):
                    if kk < KF and kk not in vd_loaded:
                        W("Y", "tT", vbuf_free[kk % 2])
                        W("Y", "cc", cc_v)
                        W("Y", "dY", v_xpage)
                        for half in range(2):
                            r, off = c256_loc(2 * kk + half)
                            vd_loaded[kk] = DMA(
                                "Y", "dY",
                                vbuf[:, (kk % 2) * 4 * D + half * 2 * D:
                                     (kk % 2) * 4 * D + half * 2 * D + 2 * D]
                                .rearrange("p (a f) -> p a f", a=2),
                                v_all[r, off:off + 256, :]
                                .rearrange("(a p) f -> p a f", p=128))
                W("T", "dY", vd_loaded[kc])
                W("T", "vV", v_pt_kc[min(kc + 1, KF - 1)])
                if kc == 0:
                    use_banks(list(range(8)))
                for kb in range(4):
                    for tile, PTb, ob_ in ((0, PT_a, 0), (1, PT_b, 4)):
                        for nb in range(4):
                            r = TMM(
                                kb == 3 and nb == 3,
                                lambda e, kc_=kc, kb_=kb, PT_=PTb, o_=ob_, nb_=nb,
                                p_=par, st=(kc == 0 and kb == 0),
                                sp=(kc == KF - 1 and kb == 3): e.matmul(
                                    ps[:, (o_ + nb_) * 512:(o_ + nb_ + 1) * 512],
                                    PT_[:, (kc_ * 4 + kb_) * 128:
                                        (kc_ * 4 + kb_ + 1) * 128],
                                    vbuf[:, p_ * 4 * D + kb_ * D + nb_ * 512:
                                         p_ * 4 * D + kb_ * D + nb_ * 512 + 512],
                                    start=st, stop=sp))
                            if r is not None:
                                tmm_last[tile] = r
                        if tile == 1 and kb == 3:
                            vbuf_free[par] = tmm_last[1]
            ev = {}
            for tile, j, ob_ in ((0, ja, 0), (1, jb, 4)):
                W("V", "tT", tmm_last[tile])
                W("V", "vV", v_rs)
                W("V", "dOut", ost_free)
                SET_V(2)
                ev[tile] = INC("V", "vV", lambda e, o_=ob_, t_=tile: e.tensor_scalar(
                    ost[:, :], ps[:, o_ * 512:(o_ + 4) * 512], p3_rs[:, t_:t_ + 1],
                    None, OP.mult))
                W("Y", "vV", ev[tile])
                ost_free = DMA("Y", "dOut", o_dram[j * 128:(j + 1) * 128, :],
                               ost[:, :])
            prev_pair_done = ev[1]
            for b in range(8):
                bank_guard[b] = ("vV", ev[0] if b < 4 else ev[1])

    BARRIER()

    if phase_cap >= 4:
        v_xload = DMA("Y", "dY", x_sb[:, :].rearrange("p (a f) -> p a f", a=NT),
                      x_dram[:, :].rearrange("(a p) f -> p a f", p=128))
        v_oT = 0
        ob_free = [0, 0]
        for i in range(NT):
            par = i % 2
            W("Y", "tT", ob_free[par])
            vo = DMA("Y", "dY", ob[:, par * D:(par + 1) * D],
                     o_dram[i * 128:(i + 1) * 128, :])
            W("T", "dY", vo)
            v_oT = transpose128(
                lambda k, p_=par: ob[:, p_ * D + k * 128: p_ * D + k * 128 + 128], 16,
                lambda e, k, psl, i_=i: e.tensor_copy(
                    oT_all[:, i_ * D + k * 128: i_ * D + k * 128 + 128], psl),
                banks=(2, 3))
            ob_free[par] = C["tT"]
        wo_free = [0, 0]
        for nb in range(4):
            par = nb % 2
            W("Y", "tT", wo_free[par])
            vd = DMA("Y", "dY",
                     woblk[:, par * KD * 512:(par + 1) * KD * 512]
                     .rearrange("p (a f) -> p a f", a=KD),
                     wob[:, nb * 512:(nb + 1) * 512]
                     .rearrange("(a p) f -> p a f", p=128))
            for i in range(NT):
                bk = i % 2
                W("T", "dY", vd)
                W("T", "vV", v_oT)
                use_banks([bk])
                for k in range(16):
                    tmm = TMM(k == 15, lambda e, k_=k, i_=i, p_=par, b_=bk,
                              st=(k == 0), sp=(k == 15): e.matmul(
                        bank(b_),
                        oT_all[:, i_ * D + k_ * 128: i_ * D + k_ * 128 + 128],
                        woblk[:, p_ * KD * 512 + k_ * 512: p_ * KD * 512 + k_ * 512 + 512],
                        start=st, stop=sp))
                if i >= 6:
                    wo_free[par] = tmm
                W("V", "tT", tmm)
                W("V", "dY", v_xload)
                SET_V(1)
                ve = INC("V", "vV", lambda e, b_=bk, i_=i, nb_=nb: e.tensor_tensor(
                    x_sb[:, i_ * D + nb_ * 512: i_ * D + nb_ * 512 + 512],
                    bank(b_),
                    x_sb[:, i_ * D + nb_ * 512: i_ * D + nb_ * 512 + 512],
                    OP.add))
                release_banks([bk], "vV", ve)

    BARRIER()

    if phase_cap >= 5:
        v_mx5, s_sq5 = {}, {}
        g5 = {}
        ready_t = {}
        g5[0] = ln_stats_group([0, 1], (p5xm, p5scr), v_mx5, s_sq5)
        ready_t[0] = quant_tile_fast(0, p5qtmp, p5qb, p5qT, g5[0])
        ready_t[1] = quant_tile_fast(1, p5qtmp, p5qb, p5qT, g5[0])
        g5[1] = ln_stats_group([2, 3], (p5xm, p5scr), v_mx5, s_sq5)
        ready_t[2] = quant_tile_fast(2, p5qtmp, p5qb, p5qT, g5[1])
        ready_t[3] = quant_tile_fast(3, p5qtmp, p5qb, p5qT, g5[1])

        w5_last = {}

        def p5_load(tag, nb):
            par = nb % 2
            W("Y", "tT", w5_last.get((tag, nb - 2), w5_last.get(("prev", par), 0)))
            return DMA(
                "Y", "dY",
                p5wblk[:, par * KD * 512:(par + 1) * KD * 512]
                .rearrange("p (a f) -> p a f", a=KD),
                wqT_dram[:, nb * 512:(nb + 1) * 512]
                .rearrange("(a p) f -> p a f", p=128))

        def p5_half(tag, tiles, interleave):
            loads = {0: p5_load(tag, 0), 1: p5_load(tag, 1)}
            for nb in range(4):
                par = nb % 2
                for i in tiles:
                    bk = i % 2
                    W("T", "dY", loads[nb])
                    W("T", "sS", ready_t[i])
                    use_banks([bk])
                    for k in range(16):
                        tmm = TMM(k == 15, lambda e, b_=bk, i_=i, k_=k, p_=par,
                                  st=(k == 0), sp=(k == 15): e.matmul(
                            bank(b_),
                            p5qT[:, i_ * D + k_ * 128: i_ * D + k_ * 128 + 128],
                            p5wblk[:, p_ * KD * 512 + k_ * 512:
                                   p_ * KD * 512 + k_ * 512 + 512],
                            start=st, stop=sp))
                    W("V", "tT", tmm)
                    SET_V(1)
                    ve = INC("V", "vV", lambda e, b_=bk, i_=i, nb_=nb:
                             e.scalar_tensor_tensor(
                        x_sb[:, i_ * D + nb_ * 512: i_ * D + nb_ * 512 + 512],
                        bank(b_), st_ct[:, i_:i_ + 1],
                        x_sb[:, i_ * D + nb_ * 512: i_ * D + nb_ * 512 + 512],
                        OP.mult, OP.add))
                    release_banks([bk], "vV", ve)
                w5_last[(tag, nb)] = tmm
                if nb + 2 <= 3:
                    loads[nb + 2] = p5_load(tag, nb + 2)
                if nb in interleave:
                    interleave[nb]()
            w5_last[("prev", 0)] = w5_last[(tag, 2)]
            w5_last[("prev", 1)] = w5_last[(tag, 3)]

        def mk_stats45():
            g5[2] = ln_stats_group([4, 5], (p5xm, p5scr), v_mx5, s_sq5)

        def mk_q(i, g):
            def f():
                ready_t[i] = quant_tile_fast(i, p5qtmp, p5qb, p5qT, g5[g])
                if i == 5:
                    g5[3] = ln_stats_group([6, 7], (p5xm, p5scr), v_mx5, s_sq5)
            return f

        p5_half("A", [0, 1, 2, 3], {0: mk_stats45, 1: mk_q(4, 2), 2: mk_q(5, 2)})
        ready_t[6] = quant_tile_fast(6, p5qtmp, p5qb, p5qT, g5[3])
        ready_t[7] = quant_tile_fast(7, p5qtmp, p5qb, p5qT, g5[3])
        p5_half("B", [4, 5, 6, 7], {})

    BARRIER()

    if os.environ.get("MK_DEBUG_XQ") == "4":
        DMA("G", "dOut", y_h[:, :], kT_shard[:, :].rearrange("a (b f) -> a b f", b=2)
            .rearrange("a b f -> (a b) f"))
    elif os.environ.get("MK_DEBUG_XQ") == "5":
        DMA("G", "dOut", y_h[:, :], q_dram[:, :].rearrange("a (b f) -> a b f", b=2)
            .rearrange("a b f -> (a b) f"))
    elif os.environ.get("MK_DEBUG_XQ") == "6":
        DMA("G", "dOut", y_h[:, :], v_shard[:, :])
    elif os.environ.get("MK_DEBUG_XQ") == "3":
        DMA("G", "dOut", y_h[:, :], o_dram[:, :])
    elif os.environ.get("MK_DEBUG_XQ") == "2":
        DMA("G", "dOut", y_h[:, :].rearrange("(a p) f -> p a f", p=128),
            wqT[:, 0:NT * D].rearrange("p (a f) -> p a f", a=NT))
    elif os.environ.get("MK_DEBUG_XQ") == "1":
        DMA("G", "dOut", y_h[:, :].rearrange("(a p) f -> p a f", p=128)[:, 0:4, :],
            xqT_all[:, 0:SEQ].rearrange("p (a f) -> p a f", a=4))
        DMA("G", "dOut", y_h[:, :].rearrange("(a p) f -> p a f", p=128)[:, 4:8, :],
            xqT_all[:, SEQ:2 * SEQ].rearrange("p (a f) -> p a f", a=4))
    else:
        DMA("Y", "dOut", y_h[:, :].rearrange("(a p) f -> p a f", p=128),
            x_sb[:, :].rearrange("p (a f) -> p a f", a=NT))
    W("Y", "dOut", C["dOut"])
    W("G", "dOut", C["dOut"])

    from contextlib import ExitStack
    with ExitStack() as ctx:
        for n in sem_names:
            sems[n] = ctx.enter_context(nc.semaphore(n))
        block = ctx.enter_context(nc.Block())
        block.tensor(lambda e: [f(e) for f in sched["T"]])
        block.vector(lambda e: [f(e) for f in sched["V"]])
        block.scalar(lambda e: [f(e) for f in sched["S"]])
        block.sync(lambda e: [f(e) for f in sched["Y"]])
        block.gpsimd(lambda e: [f(e) for f in sched["G"]])
    return nc


def make_masks():
    kidx = np.arange(SEQ)
    masks = []
    for c in range(NCORES):
        m = np.zeros((NT, 128, SEQ), dtype=np.float32)
        for j in range(NT):
            g = own4(c)[j // 2]
            gq = g * 2 + (j % 2)
            qidx = gq * 128 + np.arange(128)
            m[j] = np.where(qidx[:, None] >= kidx[None, :], 0.0, NEG)
        masks.append(m.astype(ml_dtypes.bfloat16))
    return masks


_MASKS = None
_NC_CACHE = {}


def kernel(L0_attn_x, L0_attn_qkv, L0_attn_o_proj, L0_attn_gamma, L0_attn_beta,
           L0_ffn_weight):
    global _MASKS
    x = np.ascontiguousarray(np.asarray(L0_attn_x, dtype=np.float32).reshape(SEQ, D))
    qkv = np.ascontiguousarray(np.asarray(L0_attn_qkv, dtype=np.float32))
    wo = np.ascontiguousarray(np.asarray(L0_attn_o_proj, dtype=np.float32))
    gamma = np.ascontiguousarray(np.asarray(L0_attn_gamma, dtype=np.float32))
    beta = np.ascontiguousarray(np.asarray(L0_attn_beta, dtype=np.float32))
    w = np.ascontiguousarray(np.asarray(L0_ffn_weight, dtype=np.float32))
    if _MASKS is None:
        _MASKS = make_masks()
    key = PHASE_CAP
    if key not in _NC_CACHE:
        _NC_CACHE[key] = build(key)
    nc = _NC_CACHE[key]
    in_maps = []
    for c in range(NCORES):
        xs = np.ascontiguousarray(np.concatenate(
            [x[g * 256:(g + 1) * 256] for g in own4(c)], axis=0))
        in_maps.append({
            "x_own": xs, "qkv": qkv, "o_proj": wo, "gamma": gamma, "beta": beta,
            "ffn_w": w, "mask_in": _MASKS[c],
        })
    res = run_bass_kernel_spmd(nc, in_maps, core_ids=list(range(NCORES)))
    out = np.empty((SEQ, D), dtype=np.float32)
    for c in range(NCORES):
        y = res.results[c]["y_own"]
        for sl, g in enumerate(own4(c)):
            out[g * 256:(g + 1) * 256] = y[sl * 256:(sl + 1) * 256]
    return out.reshape(1, SEQ, D)


# version 10
# speedup vs baseline: 1.1603x; 1.1603x over previous
import os
import numpy as np
import ml_dtypes
import concourse.bass as bass
import concourse.mybir as mybir
from concourse.bass_utils import run_bass_kernel_spmd

F32 = mybir.dt.float32
BF16 = mybir.dt.bfloat16
AX = mybir.AxisListType.X
OP = mybir.AluOpType
ACT = mybir.ActivationFunctionType

MAGIC = 12582912.0
EPS = 1e-5
D = 2048
SEQ = 8192
NCORES = 8
CH = 512
TPC = 1024
NT = 8
KD = 16
SCALE = float(np.float32(1.0) / np.sqrt(np.float32(2048.0)))
NEG = -1e30

ENGS = ("T", "V", "S", "Y", "G")
PHASE_CAP = int(os.environ.get("MK_PHASE_CAP", "9"))


def chunk_loc(c):
    return (c, 0) if c < 8 else (15 - c, CH)


def own4(c):
    return [c, 15 - c, 16 + c, 31 - c]


def c256_loc(j):
    if j < 8:
        return j, 0
    if j < 16:
        return 15 - j, 256
    if j < 24:
        return j - 16, 512
    return 31 - j, 768


def build(phase_cap=PHASE_CAP):
    nc = bass.Bass("TRN2", target_bir_lowering=False)

    x_h = nc.dram_tensor("x_own", [TPC, D], F32, kind="ExternalInput")
    qkv_h = nc.dram_tensor("qkv", [D, 3 * D], F32, kind="ExternalInput")
    wo_h = nc.dram_tensor("o_proj", [D, D], F32, kind="ExternalInput")
    gamma_h = nc.dram_tensor("gamma", [D], F32, kind="ExternalInput")
    beta_h = nc.dram_tensor("beta", [D], F32, kind="ExternalInput")
    w_h = nc.dram_tensor("ffn_w", [D, D], F32, kind="ExternalInput")
    mask_h = nc.dram_tensor("mask_in", [NT, 128, SEQ], BF16, kind="ExternalInput")
    y_h = nc.dram_tensor("y_own", [TPC, D], F32, kind="ExternalOutput")

    kT_shard = nc.dram_tensor("kT_shard", [D, TPC], BF16)
    v_shard = nc.dram_tensor("v_shard", [TPC, D], BF16)
    kT_all = nc.dram_tensor("kT_all", [NCORES, D, TPC], BF16, addr_space="Shared")
    v_all = nc.dram_tensor("v_all", [NCORES, TPC, D], BF16, addr_space="Shared")
    q_dram = nc.dram_tensor("q_dram", [D, TPC], BF16)
    o_dram = nc.dram_tensor("o_dram", [TPC, D], BF16)
    wqT_dram = nc.dram_tensor("wqT_dram", [D, D], BF16)
    x_dram = nc.dram_tensor("x_dram", [TPC, D], F32)

    identb_h = nc.inline_tensor(np.eye(128, dtype=ml_dtypes.bfloat16), "identb")
    identf_h = nc.inline_tensor(np.eye(16, dtype=np.float32), "identf")
    ones_col_h = nc.inline_tensor(np.ones((128, 1), dtype=np.float32), "onec")
    ones_row_h = nc.inline_tensor(np.ones((1, 128), dtype=np.float32), "oner")
    scale_col_h = nc.inline_tensor(np.full((128, 1), SCALE, dtype=np.float32), "sccol")

    alloc = nc.alloc_sbuf_tensor
    identb = alloc("sb_identb", [128, 128], BF16)
    identf = alloc("sb_identf", [16, 16], F32)
    ones_col = alloc("sb_onec", [128, 1], F32)
    ones_row = alloc("sb_oner", [1, 128], F32)
    scale_col = alloc("sb_sccol", [128, 1], F32)
    gamma_fm = alloc("gamma_fm", [128, 16], F32)
    beta_fm = alloc("beta_fm", [128, 16], F32)
    g16 = alloc("g16", [16, 128], F32)
    b16 = alloc("b16", [16, 128], F32)
    wred = alloc("wred", [128, 16], F32)
    asum = alloc("asum", [128, 1], F32)
    wtot = alloc("wtot", [1, 1], F32)
    wsc1 = alloc("wsc1", [1, 1], F32)
    rws1 = alloc("rws1", [1, 1], F32)
    ws128 = alloc("ws128", [128, 1], F32)
    rws128 = alloc("rws128", [128, 1], F32)
    st_su = alloc("st_su", [128, NT], F32)
    st_mu = alloc("st_mu", [128, NT], F32)
    st_sq = alloc("st_sq", [128, NT], F32)
    st_ve = alloc("st_ve", [128, NT], F32)
    st_rs = alloc("st_rs", [128, NT], F32)
    st_am = alloc("st_am", [128, NT], F32)
    st_as = alloc("st_as", [128, NT], F32)
    st_ct = alloc("st_ct", [128, NT], F32)
    st_s2 = alloc("st_s2", [128, NT], F32)
    st_t1 = alloc("st_t1", [128, NT], F32)
    st_t2 = alloc("st_t2", [128, NT], F32)
    p3_m = alloc("p3_m", [128, 2], F32)
    p3_mx = alloc("p3_mx", [128, 2], F32)
    p3_nm = alloc("p3_nm", [128, 2], F32)
    p3_se = alloc("p3_se", [128, 2], F32)
    p3_rs = alloc("p3_rs", [128, 2], F32)

    x_sb = alloc("x_sb", [128, NT * D], F32)
    x_base = nc.lookup_mloc(x_sb).addr

    ARENA = 141 * 1024
    arena = alloc("arena", [128, ARENA], mybir.dt.uint8)
    abase = nc.lookup_mloc(arena).addr

    def at(name, shape, dtype, off, base=None, limit=None):
        b = abase if base is None else base
        lim = ARENA if base is None else limit
        per = int(np.prod(shape[1:])) * mybir.dt.size(dtype)
        assert lim is None or off + per <= lim, (name, off, per, lim)
        return nc.alloc_sbuf_tensor_at(name, shape, dtype, offset=b + off)

    wqT = at("wqT", [128, KD * D], BF16, 0)
    wstr = at("wstr", [128, 2 * D], F32, 65536)
    xm_t = at("xm_t", [128, D], F32, 81920)
    scr2 = at("scr2", [128, D], F32, 90112)
    xqb = at("xqb", [128, D], BF16, 98304)
    xqT_all = at("xqT_all", [128, NT * D], BF16, 102400)
    qtmp = at("qtmp", [128, D], F32, 135168)
    xnT = at("xnT", [128, KD * TPC], BF16, 0)
    wblk = at("wblk", [128, 4 * KD * 512], BF16, 32768)
    qkst = at("qkst", [128, 2 * 1024], BF16, 98304)
    p2xm = at("p2xm", [128, D], F32, 102400)
    p2scr = at("p2scr", [128, D], F32, 110592)
    p2tmpb = at("p2tmpb", [128, D], BF16, 118784)
    vstage = at("vstage", [128, NT * D], BF16, 0, base=x_base, limit=65536)
    qstage = at("qstage", [128, KD * TPC], BF16, 32768, base=x_base, limit=65536)
    S_a = at("S_a", [128, SEQ], F32, 0)
    S_b = at("S_b", [128, SEQ], F32, 32768)
    PT_a = at("PT_a", [128, SEQ], BF16, 0)
    PT_b = at("PT_b", [128, SEQ], BF16, 32768)
    P_a = at("P_a", [128, SEQ], BF16, 65536)
    P_b = at("P_b", [128, SEQ], BF16, 81920)
    kbuf = at("kbuf", [128, 2 * KD * 512], BF16, 98304)
    qTt = at("qTt", [128, KD * 256], BF16, 131072)
    ost = at("ost", [128, D], BF16, 139264)
    ost_b = at("ost_b", [128, D], BF16, 81920)
    vbuf = at("vbuf", [128, 2 * 4 * D], BF16, 0, base=x_base, limit=65536)
    mb_a = at("mb_a", [128, SEQ], BF16, 32768, base=x_base, limit=65536)
    mb_b = at("mb_b", [128, SEQ], BF16, 49152, base=x_base, limit=65536)
    oT_all = at("oT_all", [128, NT * D], BF16, 0)
    ob = at("ob", [128, 2 * D], BF16, 32768)
    woblk = at("woblk", [128, 2 * KD * 512], BF16, 40960)
    p5qT = at("p5qT", [128, NT * D], BF16, 0)
    p5xm = at("p5xm", [128, D], F32, 32768)
    p5scr = at("p5scr", [128, D], F32, 40960)
    p5qb = at("p5qb", [128, D], BF16, 49152)
    p5wblk = at("p5wblk", [128, 2 * KD * 512], BF16, 53248)
    p5qtmp = at("p5qtmp", [128, D], F32, 86016)

    ps = nc.alloc_psum_tensor("ps", [128, 4096], F32)

    def bank(i, w=512):
        return ps[:, i * 512: i * 512 + w]

    sem_names = ["dY", "dG", "cc", "tT", "vV", "sS", "bar", "dOut"]
    sems = {}
    C = {n: 0 for n in sem_names}
    sched = {e: [] for e in ENGS}

    def E(eng, fn):
        sched[eng].append(fn)

    hwm = {}

    def W(eng, sem, val):
        if val and val > 0 and hwm.get((eng, sem), 0) < val:
            hwm[(eng, sem)] = val
            E(eng, lambda e, s=sem, v=val: e.wait_ge(sems[s], v))

    def DMA(eng, sem, out, in_):
        C[sem] += 16
        E(eng, lambda e, o=out, i=in_, s=sem: e.dma_start(out=o, in_=i).then_inc(sems[s], 16))
        return C[sem]

    def INC(eng, sem, fn):
        C[sem] += 1
        E(eng, lambda e, f=fn, s=sem: f(e).then_inc(sems[s], 1))
        return C[sem]

    def RAW(eng, fn):
        E(eng, fn)

    def DR(eng):
        E(eng, lambda e: e.drain())

    def BARRIER():
        tgt = C["bar"] + 5
        for eng in ENGS:
            if eng == "Y":
                W(eng, "dY", C["dY"])
                W(eng, "dOut", C["dOut"])
            if eng == "G":
                W(eng, "dG", C["dG"])
                W(eng, "cc", C["cc"])
            E(eng, lambda e: e.sem_inc(sems["bar"], 1))
            W(eng, "bar", tgt)
        C["bar"] = tgt

    bank_guard = {b: (None, 0) for b in range(8)}

    def use_banks(banks):
        for b in banks:
            sem, val = bank_guard[b]
            if sem is not None:
                W("T", sem, val)

    def release_banks(banks, sem, val):
        for b in banks:
            bank_guard[b] = (sem, val)

    jv = wred
    js = asum

    def SET_V(n=2):
        for _ in range(n):
            RAW("V", lambda e: e.tensor_copy(jv[:, 0:16], jv[:, 0:16]))

    def SET_S():
        RAW("S", lambda e: e.copy(js[:, :], js[:, :]))

    DMA("Y", "dY", identb[:, :], identb_h[:, :])
    DMA("Y", "dY", identf[:, :], identf_h[:, :])
    DMA("Y", "dY", ones_col[:, :], ones_col_h[:, :])
    DMA("Y", "dY", ones_row[:, :], ones_row_h[:, :])
    v_const = DMA("Y", "dY", scale_col[:, :], scale_col_h[:, :])
    DMA("Y", "dY", g16[:, :], gamma_h[:].rearrange("(a p) -> a p", p=128))
    v_g16 = DMA("Y", "dY", b16[:, :], beta_h[:].rearrange("(a p) -> a p", p=128))
    v_x = DMA("Y", "dY", x_sb[:, :].rearrange("p (a f) -> p a f", a=NT),
              x_h[:, :].rearrange("(a p) f -> p a f", p=128))

    W("T", "dY", v_g16)
    use_banks([6])
    t1 = INC("T", "tT", lambda e: e.matmul(bank(6, 16), g16[:, :], identf[:, :],
                                           start=True, stop=True))
    W("V", "tT", t1)
    vg = INC("V", "vV", lambda e: e.tensor_copy(gamma_fm[:, :], bank(6, 16)))
    release_banks([6], "vV", vg)
    use_banks([7])
    t2 = INC("T", "tT", lambda e: e.matmul(bank(7, 16), b16[:, :], identf[:, :],
                                           start=True, stop=True))
    W("V", "tT", t2)
    vb = INC("V", "vV", lambda e: e.tensor_copy(beta_fm[:, :], bank(7, 16)))
    release_banks([7], "vV", vb)

    w1 = {}
    for o in range(16):
        par = o % 2
        if o >= 2:
            W("Y", "vV", w1[o - 2])
        vd = DMA("Y", "dY", wstr[:, par * D:(par + 1) * D],
                 w_h[o * 128:(o + 1) * 128, :])
        W("V", "dY", vd)
        w1[o] = INC("V", "vV", lambda e, p=par, o_=o: e.tensor_reduce(
            wred[:, o_:o_ + 1], wstr[:, p * D:(p + 1) * D], AX, OP.add,
            apply_absolute_value=True))
    DR("V")
    v_as = INC("V", "vV", lambda e: e.tensor_reduce(asum[:, :], wred[:, :], AX, OP.add))
    DR("V")
    W("T", "vV", v_as)
    use_banks([0])
    tt = INC("T", "tT", lambda e: e.matmul(ps[0:1, 0:1], asum[:, :], ones_col[:, :],
                                           start=True, stop=True))
    W("V", "tT", tt)
    v0 = INC("V", "vV", lambda e: e.tensor_copy(wtot[:, :], ps[0:1, 0:1]))
    release_banks([0], "vV", v0)
    DR("V")
    v_ws = INC("V", "vV", lambda e: e.tensor_scalar(
        wsc1[:, :], wtot[:, :], 1.0 / float(D * D), 1e-6, OP.mult, OP.max))
    DR("V")
    v_rw = INC("V", "vV", lambda e: e.reciprocal(rws1[:, :], wsc1[:, :]))
    DR("V")
    W("T", "vV", v_rw)
    use_banks([1])
    tt = INC("T", "tT", lambda e: e.matmul(bank(1)[:, 0:1], ones_row[:, :], wsc1[:, :],
                                           start=True, stop=True))
    W("V", "tT", tt)
    v_b1 = INC("V", "vV", lambda e: e.tensor_copy(ws128[:, :], bank(1)[:, 0:1]))
    release_banks([1], "vV", v_b1)
    use_banks([2])
    tt = INC("T", "tT", lambda e: e.matmul(bank(2)[:, 0:1], ones_row[:, :], rws1[:, :],
                                           start=True, stop=True))
    W("V", "tT", tt)
    v_b2 = INC("V", "vV", lambda e: e.tensor_copy(rws128[:, :], bank(2)[:, 0:1]))
    release_banks([2], "vV", v_b2)
    DR("V")


    def TMM(sig, fn):
        if sig:
            return INC("T", "tT", fn)
        RAW("T", fn)
        return None

    tp_last_mm = [0]

    def transpose128(src_ap_fn, n, evac_fn, wait_v=0, banks=(6, 7), wait_s=0,
                     evac_eng="V"):
        last = 0
        esem = "vV" if evac_eng == "V" else "sS"
        for k in range(n):
            b = banks[k % 2]
            if k == 0:
                W("T", "vV", wait_v)
                W("T", "sS", wait_s)
            use_banks([b])
            tmm = INC("T", "tT", lambda e, k_=k, b_=b: e.matmul(
                ps[:, b_ * 512: b_ * 512 + 128], src_ap_fn(k_), identb[:, :],
                start=True, stop=True))
            W(evac_eng, "tT", tmm)
            if evac_eng == "V":
                SET_V(1)
            else:
                SET_S()
            last = INC(evac_eng, esem, lambda e, k_=k, b_=b: evac_fn(
                e, k_, ps[:, b_ * 512: b_ * 512 + 128]))
            release_banks([b], esem, last)
        tp_last_mm[0] = C["tT"]
        return last

    wq_done = {15: 0}
    for o in range(16):
        par = o % 2
        if o >= 2:
            W("Y", "vV", wq_done[o - 2])
        else:
            W("Y", "vV", v_as)
        vd = DMA("Y", "dY", wstr[:, par * D:(par + 1) * D],
                 w_h[o * 128:(o + 1) * 128, :])
        W("V", "dY", vd)
        W("V", "vV", v_b2)
        W("V", "tT", tp_last_mm[0])
        RAW("V", lambda e, p=par: e.tensor_scalar(
            scr2[:, :], wstr[:, p * D:(p + 1) * D], rws128[:, :], MAGIC,
            OP.mult, OP.add))
        RAW("V", lambda e: e.tensor_scalar(scr2[:, :], scr2[:, :], MAGIC, 1.0,
                                           OP.subtract, OP.min))
        v_q = INC("V", "vV", lambda e: e.tensor_scalar(
            xqb[:, :], scr2[:, :], -1.0, None, OP.max))
        wq_done[o] = transpose128(
            lambda k: xqb[:, k * 128:(k + 1) * 128], 16,
            lambda e, k, psl, o_=o: e.tensor_copy(
                wqT[:, k * D + o_ * 128: k * D + o_ * 128 + 128], psl),
            wait_v=v_q)
    v_wqT = wq_done[15]
    W("Y", "vV", v_wqT)
    for k in range(16):
        DMA("Y", "dY", wqT_dram[k * 128:(k + 1) * 128, :], wqT[:, k * D:(k + 1) * D])

    def ln_stats(i, xmbuf, scrbuf, extra_inner=False):
        RAW("V", lambda e, i_=i: e.tensor_reduce(
            st_su[:, i_:i_ + 1], x_sb[:, i_ * D:(i_ + 1) * D], AX, OP.add))
        DR("V")
        RAW("V", lambda e, i_=i: e.tensor_scalar(
            st_mu[:, i_:i_ + 1], st_su[:, i_:i_ + 1], 1.0 / D, None, OP.mult))
        DR("V")
        v_xm = INC("V", "vV", lambda e, i_=i, xb=xmbuf: e.tensor_scalar(
            xb[:, :], x_sb[:, i_ * D:(i_ + 1) * D], st_mu[:, i_:i_ + 1], None,
            OP.subtract))
        W("S", "vV", v_xm)
        s_sq = INC("S", "sS", lambda e, i_=i, xb=xmbuf, sc=scrbuf: e.activation(
            sc[:, :], xb[:, :], ACT.Square, accum_out=st_sq[:, i_:i_ + 1]))
        DR("S")
        W("V", "sS", s_sq)
        v_ve = INC("V", "vV", lambda e, i_=i: e.tensor_scalar(
            st_ve[:, i_:i_ + 1], st_sq[:, i_:i_ + 1], 1.0 / D, EPS, OP.mult, OP.add))
        DR("V")
        W("S", "vV", v_ve)
        RAW("S", lambda e, i_=i: e.activation(st_t1[:, i_:i_ + 1], st_ve[:, i_:i_ + 1],
                                              ACT.Ln))
        DR("S")
        s_rs = INC("S", "sS", lambda e, i_=i: e.activation(
            st_rs[:, i_:i_ + 1], st_t1[:, i_:i_ + 1], ACT.Exp, scale=-0.5))
        DR("S")
        if not extra_inner:
            return s_rs
        W("V", "sS", s_rs)
        RAW("V", lambda e, i_=i: e.tensor_scalar(
            st_t2[:, i_:i_ + 1], st_ve[:, i_:i_ + 1], EPS, st_rs[:, i_:i_ + 1],
            OP.subtract, OP.mult))
        DR("V")
        v_t2 = INC("V", "vV", lambda e, i_=i: e.tensor_scalar(
            st_t2[:, i_:i_ + 1], st_t2[:, i_:i_ + 1], st_rs[:, i_:i_ + 1], EPS,
            OP.mult, OP.add))
        DR("V")
        W("S", "vV", v_t2)
        RAW("S", lambda e, i_=i: e.activation(st_t1[:, i_:i_ + 1],
                                              st_t2[:, i_:i_ + 1], ACT.Ln))
        DR("S")
        s_c2 = INC("S", "sS", lambda e, i_=i: e.activation(
            st_s2[:, i_:i_ + 1], st_t1[:, i_:i_ + 1], ACT.Exp, scale=-0.5))
        DR("S")
        W("V", "sS", s_c2)
        v_s2 = INC("V", "vV", lambda e, i_=i: e.tensor_scalar(
            st_s2[:, i_:i_ + 1], st_s2[:, i_:i_ + 1], st_rs[:, i_:i_ + 1], None,
            OP.mult))
        DR("V")
        return v_s2

    def quantize_tile(i, xmbuf, scrbuf, qbuf, qT_all_buf, s_rs):
        INC("V", "vV", lambda e, i_=i, xb=xmbuf: e.tensor_reduce(
            st_am[:, i_:i_ + 1], xb[:, :], AX, OP.max, apply_absolute_value=True))
        DR("V")
        W("V", "sS", s_rs)
        RAW("V", lambda e, i_=i: e.tensor_scalar(
            st_as[:, i_:i_ + 1], st_am[:, i_:i_ + 1], st_rs[:, i_:i_ + 1], 1e-8,
            OP.mult, OP.max))
        DR("V")
        RAW("V", lambda e, i_=i: e.reciprocal(st_t2[:, i_:i_ + 1], st_as[:, i_:i_ + 1]))
        DR("V")
        RAW("V", lambda e, i_=i: e.tensor_scalar(
            st_t1[:, i_:i_ + 1], st_rs[:, i_:i_ + 1], 127.0, st_t2[:, i_:i_ + 1],
            OP.mult, OP.mult))
        DR("V")
        RAW("V", lambda e, i_=i: e.scalar_tensor_tensor(
            st_ct[:, i_:i_ + 1], st_as[:, i_:i_ + 1], 0.5 / 127.0, ws128[:, :],
            OP.mult, OP.mult))
        DR("V")
        RAW("V", lambda e, i_=i, xb=xmbuf, sc=scrbuf: e.tensor_scalar(
            sc[:, :], xb[:, :], st_t1[:, i_:i_ + 1], MAGIC, OP.mult, OP.add))
        RAW("V", lambda e, sc=scrbuf: e.tensor_scalar(
            sc[:, :], sc[:, :], MAGIC, 127.0, OP.subtract, OP.min))
        W("V", "tT", tp_last_mm[0])
        v_q = INC("V", "vV", lambda e, sc=scrbuf, qb=qbuf: e.tensor_scalar(
            qb[:, :], sc[:, :], -127.0, None, OP.max))
        return transpose128(
            lambda k, qb=qbuf: qb[:, k * 128:(k + 1) * 128], 16,
            lambda e, k, psl, i_=i, buf=qT_all_buf: e.tensor_copy(
                buf[:, i_ * D + k * 128: i_ * D + k * 128 + 128], psl),
            wait_v=v_q)

    def ln_quant_fast(i, scrbuf, qbuf, qT_all_buf):
        RAW("V", lambda e, i_=i: e.tensor_reduce(
            st_su[:, i_:i_ + 1], x_sb[:, i_ * D:(i_ + 1) * D], AX, OP.add,
            negate=True))
        DR("V")
        v_nmu = INC("V", "vV", lambda e, i_=i: e.tensor_scalar(
            st_mu[:, i_:i_ + 1], st_su[:, i_:i_ + 1], 1.0 / D, None, OP.mult))
        DR("V")
        W("S", "vV", v_nmu)
        s_sq = INC("S", "sS", lambda e, i_=i, sc=scrbuf: e.activation(
            sc[:, :], x_sb[:, i_ * D:(i_ + 1) * D], ACT.Square,
            bias=st_mu[:, i_:i_ + 1], accum_out=st_sq[:, i_:i_ + 1]))
        DR("S")
        W("V", "sS", s_sq)
        v_ve = INC("V", "vV", lambda e, i_=i: e.tensor_scalar(
            st_ve[:, i_:i_ + 1], st_sq[:, i_:i_ + 1], 1.0 / D, EPS, OP.mult, OP.add))
        DR("V")
        v_mx = INC("V", "vV", lambda e, i_=i, sc=scrbuf: e.tensor_reduce(
            st_am[:, i_:i_ + 1], sc[:, :], AX, OP.max))
        DR("V")
        W("S", "vV", v_ve)
        RAW("S", lambda e, i_=i: e.activation(st_t1[:, i_:i_ + 1], st_ve[:, i_:i_ + 1],
                                              ACT.Ln))
        DR("S")
        RAW("S", lambda e, i_=i: e.activation(
            st_rs[:, i_:i_ + 1], st_t1[:, i_:i_ + 1], ACT.Exp, scale=-0.5))
        DR("S")
        W("S", "vV", v_mx)
        RAW("S", lambda e, i_=i: e.activation(st_t2[:, i_:i_ + 1], st_am[:, i_:i_ + 1],
                                              ACT.Ln))
        DR("S")
        s_am = INC("S", "sS", lambda e, i_=i: e.activation(
            st_am[:, i_:i_ + 1], st_t2[:, i_:i_ + 1], ACT.Exp, scale=0.5))
        DR("S")
        W("V", "sS", s_am)
        RAW("V", lambda e, i_=i: e.tensor_scalar(
            st_as[:, i_:i_ + 1], st_am[:, i_:i_ + 1], st_rs[:, i_:i_ + 1], 1e-8,
            OP.mult, OP.max))
        DR("V")
        RAW("V", lambda e, i_=i: e.reciprocal(st_t2[:, i_:i_ + 1], st_as[:, i_:i_ + 1]))
        DR("V")
        RAW("V", lambda e, i_=i: e.tensor_scalar(
            st_t1[:, i_:i_ + 1], st_rs[:, i_:i_ + 1], 127.0, st_t2[:, i_:i_ + 1],
            OP.mult, OP.mult))
        DR("V")
        RAW("V", lambda e, i_=i: e.tensor_scalar(
            st_t2[:, i_:i_ + 1], st_mu[:, i_:i_ + 1], st_t1[:, i_:i_ + 1], None,
            OP.mult))
        DR("V")
        RAW("V", lambda e, i_=i: e.scalar_tensor_tensor(
            st_ct[:, i_:i_ + 1], st_as[:, i_:i_ + 1], 0.5 / 127.0, ws128[:, :],
            OP.mult, OP.mult))
        DR("V")
        RAW("V", lambda e, i_=i, sc=scrbuf: e.tensor_scalar(
            sc[:, :], x_sb[:, i_ * D:(i_ + 1) * D], st_t1[:, i_:i_ + 1],
            st_t2[:, i_:i_ + 1], OP.mult, OP.add))
        RAW("V", lambda e, sc=scrbuf: e.tensor_scalar(
            sc[:, :], sc[:, :], MAGIC, MAGIC, OP.add, OP.subtract))
        W("V", "tT", tp_last_mm[0])
        v_q = INC("V", "vV", lambda e, sc=scrbuf, qb=qbuf: e.tensor_scalar(
            qb[:, :], sc[:, :], 127.0, -127.0, OP.min, OP.max))
        return transpose128(
            lambda k, qb=qbuf: qb[:, k * 128:(k + 1) * 128], 16,
            lambda e, k, psl, i_=i, buf=qT_all_buf: e.copy(
                buf[:, i_ * D + k * 128: i_ * D + k * 128 + 128], psl),
            wait_v=v_q, evac_eng="S")

    def ln_stats_group(tiles, bufs, v_mx, s_sq):
        for i in tiles:
            RAW("V", lambda e, i_=i: e.tensor_reduce(
                st_su[:, i_:i_ + 1], x_sb[:, i_ * D:(i_ + 1) * D], AX, OP.add,
                negate=True))
        DR("V")
        v_nmu = 0
        for i in tiles:
            v_nmu = INC("V", "vV", lambda e, i_=i: e.tensor_scalar(
                st_mu[:, i_:i_ + 1], st_su[:, i_:i_ + 1], 1.0 / D, None, OP.mult))
        DR("V")
        for i in tiles:
            W("S", "vV", v_nmu)
            if i - 2 in v_mx:
                W("S", "vV", v_mx[i - 2])
            s_sq[i] = INC("S", "sS", lambda e, i_=i, sc=bufs[i % 2]: e.activation(
                sc[:, :], x_sb[:, i_ * D:(i_ + 1) * D], ACT.Square,
                bias=st_mu[:, i_:i_ + 1], accum_out=st_sq[:, i_:i_ + 1]))
            DR("S")
            W("V", "sS", s_sq[i])
            RAW("V", lambda e, i_=i: e.tensor_scalar(
                st_ve[:, i_:i_ + 1], st_sq[:, i_:i_ + 1], 1.0 / D, EPS,
                OP.mult, OP.add))
            v_mx[i] = INC("V", "vV", lambda e, i_=i, sc=bufs[i % 2]: e.tensor_reduce(
                st_am[:, i_:i_ + 1], sc[:, :], AX, OP.max))
        DR("V")
        v_ve = C["vV"]
        for i in tiles:
            W("S", "vV", v_ve)
            RAW("S", lambda e, i_=i: e.activation(st_t1[:, i_:i_ + 1],
                                                  st_ve[:, i_:i_ + 1], ACT.Ln))
        DR("S")
        for i in tiles:
            RAW("S", lambda e, i_=i: e.activation(
                st_rs[:, i_:i_ + 1], st_t1[:, i_:i_ + 1], ACT.Exp, scale=-0.5))
        DR("S")
        for i in tiles:
            RAW("S", lambda e, i_=i: e.activation(st_t2[:, i_:i_ + 1],
                                                  st_am[:, i_:i_ + 1], ACT.Ln))
        DR("S")
        s_fin = 0
        for i in tiles:
            s_fin = INC("S", "sS", lambda e, i_=i: e.activation(
                st_am[:, i_:i_ + 1], st_t2[:, i_:i_ + 1], ACT.Exp, scale=0.5))
        DR("S")
        W("V", "sS", s_fin)
        for i in tiles:
            RAW("V", lambda e, i_=i: e.tensor_scalar(
                st_as[:, i_:i_ + 1], st_am[:, i_:i_ + 1], st_rs[:, i_:i_ + 1], 1e-8,
                OP.mult, OP.max))
        DR("V")
        for i in tiles:
            RAW("V", lambda e, i_=i: e.reciprocal(st_t2[:, i_:i_ + 1],
                                                  st_as[:, i_:i_ + 1]))
        DR("V")
        for i in tiles:
            RAW("V", lambda e, i_=i: e.tensor_scalar(
                st_t1[:, i_:i_ + 1], st_rs[:, i_:i_ + 1], 127.0, st_t2[:, i_:i_ + 1],
                OP.mult, OP.mult))
        DR("V")
        for i in tiles:
            RAW("V", lambda e, i_=i: e.tensor_scalar(
                st_su[:, i_:i_ + 1], st_mu[:, i_:i_ + 1], st_t1[:, i_:i_ + 1], None,
                OP.mult))
        for i in tiles:
            RAW("V", lambda e, i_=i: e.scalar_tensor_tensor(
                st_ct[:, i_:i_ + 1], st_as[:, i_:i_ + 1], 0.5 / 127.0, ws128[:, :],
                OP.mult, OP.mult))
        DR("V")
        return INC("V", "vV", lambda e: e.engine_nop())

    def quant_tile_fast(i, qt, qbuf, qT_all_buf, v_stats):
        W("V", "vV", v_stats)
        RAW("V", lambda e, i_=i, sc=qt: e.tensor_scalar(
            sc[:, :], x_sb[:, i_ * D:(i_ + 1) * D], st_t1[:, i_:i_ + 1],
            st_su[:, i_:i_ + 1], OP.mult, OP.add))
        RAW("V", lambda e, sc=qt: e.tensor_scalar(
            sc[:, :], sc[:, :], MAGIC, MAGIC, OP.add, OP.subtract))
        W("V", "tT", tp_last_mm[0])
        v_q = INC("V", "vV", lambda e, sc=qt, qb=qbuf: e.tensor_scalar(
            qb[:, :], sc[:, :], 127.0, -127.0, OP.min, OP.max))
        return transpose128(
            lambda k, qb=qbuf: qb[:, k * 128:(k + 1) * 128], 16,
            lambda e, k, psl, i_=i, buf=qT_all_buf: e.copy(
                buf[:, i_ * D + k * 128: i_ * D + k * 128 + 128], psl),
            wait_v=v_q, evac_eng="S")

    if phase_cap >= 1:
        W("V", "dY", v_x)
        W("S", "dY", v_x)
        v_mx1, s_sq1 = {}, {}
        grp_stats = {}
        grp_stats[0] = ln_stats_group([0, 1], (xm_t, scr2), v_mx1, s_sq1)
        for i in range(NT):
            ready = quant_tile_fast(i, qtmp, xqb, xqT_all, grp_stats[i // 2])
            if i % 2 == 1 and i + 1 < NT:
                grp_stats[i // 2 + 1] = ln_stats_group(
                    [i + 1, i + 2], (xm_t, scr2), v_mx1, s_sq1)
            bk = i % 2
            W("T", "sS", ready)
            for nb in range(4):
                use_banks([bk])
                for k in range(16):
                    tmm = TMM(k == 15, lambda e, b_=bk, i_=i, k_=k, nb_=nb,
                              st=(k == 0), sp=(k == 15): e.matmul(
                        bank(b_),
                        xqT_all[:, i_ * D + k_ * 128: i_ * D + k_ * 128 + 128],
                        wqT[:, k_ * D + nb_ * 512: k_ * D + nb_ * 512 + 512],
                        start=st, stop=sp))
                W("V", "tT", tmm)
                SET_V(1)
                ve = INC("V", "vV", lambda e, b_=bk, i_=i, nb_=nb:
                         e.scalar_tensor_tensor(
                    x_sb[:, i_ * D + nb_ * 512: i_ * D + nb_ * 512 + 512], bank(b_),
                    st_ct[:, i_:i_ + 1],
                    x_sb[:, i_ * D + nb_ * 512: i_ * D + nb_ * 512 + 512],
                    OP.mult, OP.add))
                release_banks([bk], "vV", ve)

    BARRIER()

    cc_k = cc_v = 0
    if phase_cap >= 2:
        v_xnT = 0
        for i in range(NT):
            v_s2 = ln_stats(i, p2xm, p2scr, extra_inner=True)
            W("V", "tT", tp_last_mm[0])
            v_tmpb = INC("V", "vV", lambda e, i_=i: e.tensor_scalar(
                p2tmpb[:, :], p2xm[:, :], st_s2[:, i_:i_ + 1], None, OP.mult))
            INC("V", "vV", lambda e, i_=i: e.scalar_tensor_tensor(
                x_sb[:, i_ * D:(i_ + 1) * D], p2xm[:, :], st_rs[:, i_:i_ + 1],
                x_sb[:, i_ * D:(i_ + 1) * D], OP.mult, OP.add))
            v_xnT = transpose128(
                lambda k: p2tmpb[:, k * 128:(k + 1) * 128], 16,
                lambda e, k, psl, i_=i: e.tensor_scalar(
                    xnT[:, k * TPC + i_ * 128: k * TPC + i_ * 128 + 128], psl,
                    gamma_fm[:, k:k + 1], beta_fm[:, k:k + 1], OP.mult, OP.add),
                wait_v=v_tmpb)

        W("Y", "vV", C["vV"])
        v_xpage = DMA("Y", "dY", x_dram[:, :].rearrange("(a p) f -> p a f", p=128),
                      x_sb[:, :].rearrange("p (a f) -> p a f", a=NT))

        stage_free = [0, 0]
        wblk_free = [0, 0]

        def qk_block(m, row, out_dram):
            par = m % 2
            W("G", "tT", wblk_free[par])
            vd = DMA("G", "dG",
                     wblk[:, par * KD * 512: par * KD * 512 + KD * 128]
                     .rearrange("p (a f) -> p a f", a=KD),
                     qkv_h[:, m * 128:(m + 1) * 128]
                     .rearrange("(a p) f -> p a f", p=128))
            W("T", "dG", vd)
            W("T", "vV", v_xnT)
            banks2 = (0, 1) if par == 0 else (2, 3)
            use_banks(banks2)
            tmm = 0
            for h in range(2):
                for k in range(16):
                    tmm = TMM(h == 1 and k == 15, lambda e, k_=k, h_=h, p_=par,
                              st=(k == 0), sp=(k == 15): e.matmul(
                        ps[:, p_ * 1024 + h_ * 512: p_ * 1024 + h_ * 512 + 512],
                        wblk[:, p_ * KD * 512 + k_ * 128: p_ * KD * 512 + k_ * 128 + 128],
                        xnT[:, k_ * TPC + h_ * 512: k_ * TPC + h_ * 512 + 512],
                        start=st, stop=sp))
            wblk_free[par] = tmm
            W("S", "tT", tmm)
            W("S", "dY", stage_free[par])
            SET_S()
            se = INC("S", "sS", lambda e, p_=par: e.copy(
                qkst[:, p_ * 1024:(p_ + 1) * 1024], ps[:, p_ * 1024:(p_ + 1) * 1024]))
            release_banks(banks2, "sS", se)
            W("Y", "sS", se)
            vy = DMA("Y", "dY", out_dram[row * 128:(row + 1) * 128, :],
                     qkst[:, par * 1024:(par + 1) * 1024])
            stage_free[par] = vy
            return vy

        for m in range(16):
            qk_block(16 + m, m, kT_shard)

        wv_loads = 0
        for nb in range(4):
            W("G", "tT", wblk_free[nb % 2] if nb < 2 else 0)
            wv_loads = DMA("G", "dG",
                           wblk[:, nb * KD * 512:(nb + 1) * KD * 512]
                           .rearrange("p (a f) -> p a f", a=KD),
                           qkv_h[:, 2 * D + nb * 512: 2 * D + nb * 512 + 512]
                           .rearrange("(a p) f -> p a f", p=128))

        if os.environ.get("MK_SKIP_AG") != "1":
            W("G", "dY", C["dY"])
            W("G", "dG", C["dG"])
            C["cc"] += 1
            cc_k = C["cc"]
            E("G", lambda e: e.collective_compute(
                "AllGather", OP.bypass, replica_groups=[list(range(NCORES))],
                ins=[kT_shard[:, :]], outs=[kT_all[:, :, :]]).then_inc(sems["cc"], 1))

        for nb in range(4):
            for i in range(NT):
                bk = i % 2
                W("T", "dG", wv_loads)
                W("T", "vV", v_xnT)
                use_banks([bk])
                for k in range(16):
                    tmm = TMM(k == 15, lambda e, k_=k, i_=i, nb_=nb, b_=bk,
                              st=(k == 0), sp=(k == 15): e.matmul(
                        bank(b_),
                        xnT[:, k_ * TPC + i_ * 128: k_ * TPC + i_ * 128 + 128],
                        wblk[:, nb_ * KD * 512 + k_ * 512: nb_ * KD * 512 + k_ * 512 + 512],
                        start=st, stop=sp))
                W("S", "tT", tmm)
                W("S", "dY", v_xpage)
                SET_S()
                se = INC("S", "sS", lambda e, b_=bk, i_=i, nb_=nb: e.copy(
                    vstage[:, i_ * D + nb_ * 512: i_ * D + nb_ * 512 + 512], bank(b_)))
                release_banks([bk], "sS", se)
        v_vdone = C["sS"]

        W("Y", "sS", v_vdone)
        W("Y", "cc", cc_k)
        v_vst = DMA("Y", "dY", v_shard[:, :].rearrange("(a p) f -> p a f", p=128),
                    vstage[:, :].rearrange("p (a f) -> p a f", a=NT))
        wq_loads = 0
        for nb in range(4):
            W("G", "cc", cc_k)
            W("G", "tT", C["tT"])
            wq_loads = DMA("G", "dG",
                           wblk[:, nb * KD * 512:(nb + 1) * KD * 512]
                           .rearrange("p (a f) -> p a f", a=KD),
                           qkv_h[:, nb * 512: nb * 512 + 512]
                           .rearrange("(a p) f -> p a f", p=128))

        if os.environ.get("MK_SKIP_AG") != "1":
            W("G", "dY", C["dY"])
            W("G", "dG", C["dG"])
            C["cc"] += 1
            cc_v = C["cc"]
            E("G", lambda e: e.collective_compute(
                "AllGather", OP.bypass, replica_groups=[list(range(NCORES))],
                ins=[v_shard[:, :]], outs=[v_all[:, :, :]]).then_inc(sems["cc"], 1))

        for m in range(16):
            par = m % 2
            banks2 = (0, 1) if par == 0 else (2, 3)
            W("T", "dG", wq_loads)
            use_banks(banks2)
            for h in range(2):
                for k in range(16):
                    tmm = TMM(h == 1 and k == 15, lambda e, k_=k, h_=h, m_=m, p_=par,
                              st=(k == 0), sp=(k == 15): e.matmul(
                        ps[:, p_ * 1024 + h_ * 512: p_ * 1024 + h_ * 512 + 512],
                        wblk[:, (m_ // 4) * KD * 512 + k_ * 512 + (m_ % 4) * 128:
                             (m_ // 4) * KD * 512 + k_ * 512 + (m_ % 4) * 128 + 128],
                        xnT[:, k_ * TPC + h_ * 512: k_ * TPC + h_ * 512 + 512],
                        start=st, stop=sp))
            W("S", "tT", tmm)
            SET_S()
            se = INC("S", "sS", lambda e, m_=m, p_=par: e.copy(
                qstage[:, m_ * TPC:(m_ + 1) * TPC], ps[:, p_ * 1024:(p_ + 1) * 1024]))
            release_banks(banks2, "sS", se)

        W("Y", "sS", C["sS"])
        W("Y", "cc", cc_v)
        DMA("Y", "dY", q_dram[:, :].rearrange("(a p) f -> p a f", p=128),
            qstage[:, :].rearrange("p (a f) -> p a f", a=KD))

    BARRIER()

    if phase_cap >= 3:
        pairs = [(0, 1, 4), (2, 3, 8), (4, 5, 12), (6, 7, 16)]
        kbuf_free = [0, 0]
        vbuf_free = [0, 0]
        qTt_free = 0
        mb_free = 0
        ost_free = [0, 0]
        prev_pair_done = 0

        for (ja, jb, KF) in pairs:
            W("Y", "tT", qTt_free)
            vq = DMA("Y", "dY", qTt[:, :].rearrange("p (a f) -> p a f", a=KD),
                     q_dram[:, ja * 128: ja * 128 + 256]
                     .rearrange("(a p) f -> p a f", p=128))
            W("Y", "sS", mb_free)
            vma = DMA("Y", "dY", mb_a[:, 0:KF * 512], mask_h[ja, :, 0:KF * 512])
            vmb = DMA("Y", "dY", mb_b[:, 0:KF * 512], mask_h[jb, :, 0:KF * 512])
            kd_loaded = {}
            sevac_hist = {}
            for kc in range(KF):
                par = kc % 2
                for kk in (kc, kc + 1):
                    if kk < KF and kk not in kd_loaded:
                        W("Y", "tT", kbuf_free[kk % 2])
                        W("Y", "cc", cc_v)
                        for half in range(2):
                            r, off = c256_loc(2 * kk + half)
                            kd_loaded[kk] = DMA(
                                "Y", "dY",
                                kbuf[:, (kk % 2) * KD * 512:
                                     ((kk % 2) + 1) * KD * 512]
                                .rearrange("p (a f) -> p a f", f=512)
                                [:, :, half * 256:(half + 1) * 256],
                                kT_all[r, :, off:off + 256]
                                .rearrange("(a p) f -> p a f", p=128))
                W("T", "dY", kd_loaded[kc])
                W("T", "dY", max(vq, vma, vmb))
                for (tile, qoff, mb, psb) in ((0, 0, mb_a, 0), (1, 128, mb_b, 4)):
                    bk = psb + par
                    use_banks([bk])
                    for k in range(16):
                        RAW("T", lambda e, k_=k, b_=bk, q_=qoff, p_=par: e.matmul(
                            bank(b_),
                            qTt[:, k_ * 256 + q_: k_ * 256 + q_ + 128],
                            kbuf[:, p_ * KD * 512 + k_ * 512: p_ * KD * 512 + k_ * 512 + 512],
                            start=(k_ == 0), stop=False))
                    tmm = INC("T", "tT", lambda e, b_=bk, m_=mb, kc_=kc: e.matmul(
                        bank(b_), identb[:, :], m_[:, kc_ * 512:(kc_ + 1) * 512],
                        start=False, stop=True))
                    if tile == 1:
                        kbuf_free[par] = tmm
                        if kc == KF - 1:
                            qTt_free = tmm
                    Sbuf = S_a if tile == 0 else S_b
                    W("S", "tT", tmm)
                    SET_S()
                    se = INC("S", "sS", lambda e, b_=bk, kc_=kc, S_=Sbuf: e.copy(
                        S_[:, kc_ * 512:(kc_ + 1) * 512], bank(b_)))
                    release_banks([bk], "sS", se)
                    sevac_hist[(tile, kc)] = se
                    W("V", "sS", se)
                    if kc == 0:
                        INC("V", "vV", lambda e, S_=Sbuf, t_=tile: e.tensor_reduce(
                            p3_m[:, t_:t_ + 1], S_[:, 0:512], AX, OP.max))
                        DR("V")
                    else:
                        RAW("V", lambda e, kc_=kc, S_=Sbuf, t_=tile: e.tensor_reduce(
                            p3_mx[:, t_:t_ + 1], S_[:, kc_ * 512:(kc_ + 1) * 512],
                            AX, OP.max))
                        DR("V")
                        INC("V", "vV", lambda e, t_=tile: e.tensor_max(
                            p3_m[:, t_:t_ + 1], p3_m[:, t_:t_ + 1],
                            p3_mx[:, t_:t_ + 1]))
                        DR("V")
            mb_free = sevac_hist[(1, KF - 1)]
            v_nm = INC("V", "vV", lambda e: e.tensor_scalar(
                p3_nm[:, :], p3_m[:, :], -SCALE, None, OP.mult))
            DR("V")
            W("S", "vV", v_nm)
            s_expa = INC("S", "sS", lambda e, KF_=KF: e.activation(
                P_a[:, 0:KF_ * 512], S_a[:, 0:KF_ * 512], ACT.Exp,
                bias=p3_nm[:, 0:1], scale=scale_col[:, :], accum_out=p3_se[:, 0:1]))
            DR("S")
            W("S", "dOut", ost_free[1])
            s_expb = INC("S", "sS", lambda e, KF_=KF: e.activation(
                P_b[:, 0:KF_ * 512], S_b[:, 0:KF_ * 512], ACT.Exp,
                bias=p3_nm[:, 1:2], scale=scale_col[:, :], accum_out=p3_se[:, 1:2]))
            DR("S")
            W("V", "sS", s_expb)
            v_rs = INC("V", "vV", lambda e: e.reciprocal(p3_rs[:, :], p3_se[:, :]))
            DR("V")
            nblk = KF * 4
            v_pt = 0
            v_pt_kc = {}
            for m in range(2 * nblk):
                tile = m // nblk
                blk = m % nblk
                Pb = P_a if tile == 0 else P_b
                PTb = PT_a if tile == 0 else PT_b
                b8 = m % 8
                if m == 0:
                    W("T", "sS", s_expa)
                    W("T", "vV", prev_pair_done)
                elif tile == 1 and blk == 0:
                    W("T", "sS", s_expb)
                use_banks([b8])
                tmm = INC("T", "tT", lambda e, b_=b8, P_=Pb, blk_=blk: e.matmul(
                    ps[:, b_ * 512: b_ * 512 + 128],
                    P_[:, blk_ * 128:(blk_ + 1) * 128], identb[:, :],
                    start=True, stop=True))
                W("V", "tT", tmm)
                SET_V(1)
                v_pt = INC("V", "vV", lambda e, b_=b8, PT_=PTb, blk_=blk:
                           e.tensor_copy(
                    PT_[:, blk_ * 128:(blk_ + 1) * 128],
                    ps[:, b_ * 512: b_ * 512 + 128]))
                release_banks([b8], "vV", v_pt)
                if tile == 1:
                    v_pt_kc[blk // 4] = v_pt
            vd_loaded = {}
            tmm_last = {0: 0, 1: 0}
            for kc in range(KF):
                par = kc % 2
                for kk in (kc, kc + 1):
                    if kk < KF and kk not in vd_loaded:
                        W("Y", "tT", vbuf_free[kk % 2])
                        W("Y", "cc", cc_v)
                        W("Y", "dY", v_xpage)
                        for half in range(2):
                            r, off = c256_loc(2 * kk + half)
                            vd_loaded[kk] = DMA(
                                "Y", "dY",
                                vbuf[:, (kk % 2) * 4 * D + half * 2 * D:
                                     (kk % 2) * 4 * D + half * 2 * D + 2 * D]
                                .rearrange("p (a f) -> p a f", a=2),
                                v_all[r, off:off + 256, :]
                                .rearrange("(a p) f -> p a f", p=128))
                W("T", "dY", vd_loaded[kc])
                W("T", "vV", v_pt_kc[min(kc + 1, KF - 1)])
                if kc == 0:
                    use_banks(list(range(8)))
                for kb in range(4):
                    for tile, PTb, ob_ in ((0, PT_a, 0), (1, PT_b, 4)):
                        for nb in range(4):
                            r = TMM(
                                kb == 3 and nb == 3,
                                lambda e, kc_=kc, kb_=kb, PT_=PTb, o_=ob_, nb_=nb,
                                p_=par, st=(kc == 0 and kb == 0),
                                sp=(kc == KF - 1 and kb == 3): e.matmul(
                                    ps[:, (o_ + nb_) * 512:(o_ + nb_ + 1) * 512],
                                    PT_[:, (kc_ * 4 + kb_) * 128:
                                        (kc_ * 4 + kb_ + 1) * 128],
                                    vbuf[:, p_ * 4 * D + kb_ * D + nb_ * 512:
                                         p_ * 4 * D + kb_ * D + nb_ * 512 + 512],
                                    start=st, stop=sp))
                            if r is not None:
                                tmm_last[tile] = r
                        if tile == 1 and kb == 3:
                            vbuf_free[par] = tmm_last[1]
            ev = {}
            for tile, j, ob_ in ((0, ja, 0), (1, jb, 4)):
                buf = ost if tile == 0 else ost_b
                W("V", "tT", tmm_last[tile])
                W("V", "vV", v_rs)
                W("V", "dOut", ost_free[tile])
                SET_V(2)
                ev[tile] = INC("V", "vV", lambda e, o_=ob_, t_=tile, b_=buf:
                               e.tensor_scalar(
                    b_[:, :], ps[:, o_ * 512:(o_ + 4) * 512], p3_rs[:, t_:t_ + 1],
                    None, OP.mult))
                W("Y", "vV", ev[tile])
                ost_free[tile] = DMA("Y", "dOut", o_dram[j * 128:(j + 1) * 128, :],
                                     buf[:, :])
            prev_pair_done = ev[1]
            for b in range(8):
                bank_guard[b] = ("vV", ev[0] if b < 4 else ev[1])

    BARRIER()

    if phase_cap >= 4:
        v_xload = DMA("Y", "dY", x_sb[:, :].rearrange("p (a f) -> p a f", a=NT),
                      x_dram[:, :].rearrange("(a p) f -> p a f", p=128))
        v_oT = 0
        ob_free = [0, 0]
        for i in range(NT):
            par = i % 2
            W("Y", "tT", ob_free[par])
            vo = DMA("Y", "dY", ob[:, par * D:(par + 1) * D],
                     o_dram[i * 128:(i + 1) * 128, :])
            W("T", "dY", vo)
            v_oT = transpose128(
                lambda k, p_=par: ob[:, p_ * D + k * 128: p_ * D + k * 128 + 128], 16,
                lambda e, k, psl, i_=i: e.tensor_copy(
                    oT_all[:, i_ * D + k * 128: i_ * D + k * 128 + 128], psl),
                banks=(2, 3))
            ob_free[par] = C["tT"]
        wo_free = [0, 0]
        for nb in range(4):
            par = nb % 2
            W("G", "tT", wo_free[par])
            vd = DMA("G", "dG",
                     woblk[:, par * KD * 512:(par + 1) * KD * 512]
                     .rearrange("p (a f) -> p a f", a=KD),
                     wo_h[:, nb * 512:(nb + 1) * 512]
                     .rearrange("(a p) f -> p a f", p=128))
            for i in range(NT):
                bk = i % 2
                W("T", "dG", vd)
                W("T", "vV", v_oT)
                use_banks([bk])
                for k in range(16):
                    tmm = TMM(k == 15, lambda e, k_=k, i_=i, p_=par, b_=bk,
                              st=(k == 0), sp=(k == 15): e.matmul(
                        bank(b_),
                        oT_all[:, i_ * D + k_ * 128: i_ * D + k_ * 128 + 128],
                        woblk[:, p_ * KD * 512 + k_ * 512: p_ * KD * 512 + k_ * 512 + 512],
                        start=st, stop=sp))
                if i >= 6:
                    wo_free[par] = tmm
                W("V", "tT", tmm)
                W("V", "dY", v_xload)
                SET_V(1)
                ve = INC("V", "vV", lambda e, b_=bk, i_=i, nb_=nb: e.tensor_tensor(
                    x_sb[:, i_ * D + nb_ * 512: i_ * D + nb_ * 512 + 512],
                    bank(b_),
                    x_sb[:, i_ * D + nb_ * 512: i_ * D + nb_ * 512 + 512],
                    OP.add))
                release_banks([bk], "vV", ve)

    BARRIER()

    if phase_cap >= 5:
        v_mx5, s_sq5 = {}, {}
        g5 = {}
        g5[0] = ln_stats_group([0, 1], (p5xm, p5scr), v_mx5, s_sq5)
        ready_t = {}
        for i in range(NT):
            ready_t[i] = quant_tile_fast(i, p5qtmp, p5qb, p5qT, g5[i // 2])
            if i % 2 == 1 and i + 1 < NT:
                g5[i // 2 + 1] = ln_stats_group([i + 1, i + 2], (p5xm, p5scr),
                                                v_mx5, s_sq5)
        ready = ready_t[NT - 1]
        w5_loaded = {}
        w5_last_mm = {}

        def load_w5(nb):
            if nb > 3 or nb in w5_loaded:
                return
            par = nb % 2
            W("G", "tT", w5_last_mm.get(nb - 2, 0))
            w5_loaded[nb] = DMA(
                "G", "dG",
                p5wblk[:, par * KD * 512:(par + 1) * KD * 512]
                .rearrange("p (a f) -> p a f", a=KD),
                wqT_dram[:, nb * 512:(nb + 1) * 512]
                .rearrange("(a p) f -> p a f", p=128))

        load_w5(0)
        load_w5(1)
        for nb in range(4):
            par = nb % 2
            for i in range(NT):
                bk = i % 2
                W("T", "dG", w5_loaded[nb])
                if nb == 0 and i == 0:
                    W("T", "sS", ready)
                use_banks([bk])
                for k in range(16):
                    tmm = TMM(k == 15, lambda e, b_=bk, i_=i, k_=k, p_=par,
                              st=(k == 0), sp=(k == 15): e.matmul(
                        bank(b_),
                        p5qT[:, i_ * D + k_ * 128: i_ * D + k_ * 128 + 128],
                        p5wblk[:, p_ * KD * 512 + k_ * 512: p_ * KD * 512 + k_ * 512 + 512],
                        start=st, stop=sp))
                W("V", "tT", tmm)
                ve = INC("V", "vV", lambda e, b_=bk, i_=i, nb_=nb:
                         e.scalar_tensor_tensor(
                    x_sb[:, i_ * D + nb_ * 512: i_ * D + nb_ * 512 + 512], bank(b_),
                    st_ct[:, i_:i_ + 1],
                    x_sb[:, i_ * D + nb_ * 512: i_ * D + nb_ * 512 + 512],
                    OP.mult, OP.add))
                release_banks([bk], "vV", ve)
            w5_last_mm[nb] = tmm
            load_w5(nb + 2)

    BARRIER()

    if os.environ.get("MK_DEBUG_XQ") == "4":
        DMA("G", "dOut", y_h[:, :], kT_shard[:, :].rearrange("a (b f) -> a b f", b=2)
            .rearrange("a b f -> (a b) f"))
    elif os.environ.get("MK_DEBUG_XQ") == "5":
        DMA("G", "dOut", y_h[:, :], q_dram[:, :].rearrange("a (b f) -> a b f", b=2)
            .rearrange("a b f -> (a b) f"))
    elif os.environ.get("MK_DEBUG_XQ") == "6":
        DMA("G", "dOut", y_h[:, :], v_shard[:, :])
    elif os.environ.get("MK_DEBUG_XQ") == "3":
        DMA("G", "dOut", y_h[:, :], o_dram[:, :])
    elif os.environ.get("MK_DEBUG_XQ") == "2":
        DMA("G", "dOut", y_h[:, :].rearrange("(a p) f -> p a f", p=128),
            wqT[:, 0:NT * D].rearrange("p (a f) -> p a f", a=NT))
    elif os.environ.get("MK_DEBUG_XQ") == "1":
        DMA("G", "dOut", y_h[:, :].rearrange("(a p) f -> p a f", p=128)[:, 0:4, :],
            xqT_all[:, 0:SEQ].rearrange("p (a f) -> p a f", a=4))
        DMA("G", "dOut", y_h[:, :].rearrange("(a p) f -> p a f", p=128)[:, 4:8, :],
            xqT_all[:, SEQ:2 * SEQ].rearrange("p (a f) -> p a f", a=4))
    else:
        DMA("Y", "dOut", y_h[:, :].rearrange("(a p) f -> p a f", p=128),
            x_sb[:, :].rearrange("p (a f) -> p a f", a=NT))
    W("Y", "dOut", C["dOut"])
    W("G", "dOut", C["dOut"])

    from contextlib import ExitStack
    with ExitStack() as ctx:
        for n in sem_names:
            sems[n] = ctx.enter_context(nc.semaphore(n))
        block = ctx.enter_context(nc.Block())
        block.tensor(lambda e: [f(e) for f in sched["T"]])
        block.vector(lambda e: [f(e) for f in sched["V"]])
        block.scalar(lambda e: [f(e) for f in sched["S"]])
        block.sync(lambda e: [f(e) for f in sched["Y"]])
        block.gpsimd(lambda e: [f(e) for f in sched["G"]])
    return nc


def make_masks():
    kidx = np.arange(SEQ)
    masks = []
    for c in range(NCORES):
        m = np.zeros((NT, 128, SEQ), dtype=np.float32)
        for j in range(NT):
            g = own4(c)[j // 2]
            gq = g * 2 + (j % 2)
            qidx = gq * 128 + np.arange(128)
            m[j] = np.where(qidx[:, None] >= kidx[None, :], 0.0, NEG)
        masks.append(m.astype(ml_dtypes.bfloat16))
    return masks


_MASKS = None
_NC_CACHE = {}


def kernel(L0_attn_x, L0_attn_qkv, L0_attn_o_proj, L0_attn_gamma, L0_attn_beta,
           L0_ffn_weight):
    global _MASKS
    x = np.ascontiguousarray(np.asarray(L0_attn_x, dtype=np.float32).reshape(SEQ, D))
    qkv = np.ascontiguousarray(np.asarray(L0_attn_qkv, dtype=np.float32))
    wo = np.ascontiguousarray(np.asarray(L0_attn_o_proj, dtype=np.float32))
    gamma = np.ascontiguousarray(np.asarray(L0_attn_gamma, dtype=np.float32))
    beta = np.ascontiguousarray(np.asarray(L0_attn_beta, dtype=np.float32))
    w = np.ascontiguousarray(np.asarray(L0_ffn_weight, dtype=np.float32))
    if _MASKS is None:
        _MASKS = make_masks()
    key = PHASE_CAP
    if key not in _NC_CACHE:
        _NC_CACHE[key] = build(key)
    nc = _NC_CACHE[key]
    in_maps = []
    for c in range(NCORES):
        xs = np.ascontiguousarray(np.concatenate(
            [x[g * 256:(g + 1) * 256] for g in own4(c)], axis=0))
        in_maps.append({
            "x_own": xs, "qkv": qkv, "o_proj": wo, "gamma": gamma, "beta": beta,
            "ffn_w": w, "mask_in": _MASKS[c],
        })
    res = run_bass_kernel_spmd(nc, in_maps, core_ids=list(range(NCORES)))
    out = np.empty((SEQ, D), dtype=np.float32)
    for c in range(NCORES):
        y = res.results[c]["y_own"]
        for sl, g in enumerate(own4(c)):
            out[g * 256:(g + 1) * 256] = y[sl * 256:(sl + 1) * 256]
    return out.reshape(1, SEQ, D)
